# version 08
# baseline (speedup 1.0000x reference)
import math
import contextlib
import numpy as np
import ml_dtypes
import concourse.bass as bass
import concourse.mybir as mybir
from concourse.bass import IndirectOffsetOnAxis
from concourse.bass_utils import run_bass_kernel_spmd

F32 = mybir.dt.float32
BF16 = mybir.dt.bfloat16
U32 = mybir.dt.uint32
OP = mybir.AluOpType
AF = mybir.ActivationFunctionType
AX = mybir.AxisListType

NCORES = 8
N = 4096
C = 1024
H = 16
E = 8
DI = 4096
TILE = 512
NT = N // TILE
NSUB = TILE // 128
NBLK = N // 128
CAP = 1280
NSB = CAP // 128
KTOP = 8192
NBIS = 38
EPS = 1e-6
EPSD = 256e-6
LNEPS = 1e-5

ENGS = ("tensor", "vector", "scalar", "gpsimd", "sync")


def load_act_set(s, idx):
    return s.add_instruction(mybir.InstLoadActFuncSet(
        name=s.bass.get_next_instruction_name(), act_func_set_id=idx, ins=[], outs=[]))


class Plan:
    def __init__(self, nc, ctx):
        self.nc = nc
        self.ops = {e: [] for e in ENGS}
        self.cnt = {}
        self.sems = {}
        self.tag = "ph0"
        for s in ("dS", "dG", "dP", "dV", "eT", "eV", "eS", "cc"):
            self.sems[s] = ctx.enter_context(nc.semaphore(s))
            self.cnt[s] = 0

    def op(self, eng, fn, wait=(), inc=None, amt=None):
        wt = []
        for item in wait:
            if isinstance(item, tuple):
                if item[1] > 0:
                    wt.append((item[0], item[1]))
            else:
                if self.cnt[item] > 0:
                    wt.append((item, self.cnt[item]))
        if inc is not None:
            if amt is None:
                amt = 16 if inc.startswith("d") else 1
            self.cnt[inc] += amt
        self.ops[eng].append((fn, wt, inc, amt, self.tag))
        return self.cnt[inc] if inc else None

    def replay(self, block):
        sems = self.sems
        for ename in ENGS:
            oplist = self.ops[ename]

            def body(e, oplist=oplist, ename=ename):
                for fn, wt, inc, amt, tag in oplist:
                    for s, v in wt:
                        e.wait_ge(sems[s], v)
                    with self.nc.named_scope(f"{tag}.{ename}"):
                        instr = fn(e)
                    if inc is not None:
                        instr.then_inc(sems[inc], amt)
            getattr(block, ename)(body)


def build_nc(debug=False, stop_after=None):
    nc = bass.Bass("TRN2", target_bir_lowering=False, detect_race_conditions=False)
    dt = nc.dram_tensor

    x_d = dt("x", [N, C], F32, kind="ExternalInput").ap()
    wqkv_d = dt("wqkv_l", [128, 24 * 8 * 128], F32, kind="ExternalInput").ap()
    bqkv_d = dt("bqkv", [1, 3 * C], F32, kind="ExternalInput").ap()
    wout_d = dt("wout_l", [128, 8 * 8 * 128], F32, kind="ExternalInput").ap()
    bout_d = dt("bout", [1, C], F32, kind="ExternalInput").ap()
    projp_d = dt("projpad", [128, H * 2 * 128], F32, kind="ExternalInput").ap()
    wr_d = dt("wr_l", [128, 8 * E], F32, kind="ExternalInput").ap()
    w1_d = dt("w1_l", [128, E * 32768], F32, kind="ExternalInput").ap()
    b1_d = dt("b1_l", [1, E * DI], F32, kind="ExternalInput").ap()
    w2_d = dt("w2_l", [128, E * 32768], F32, kind="ExternalInput").ap()
    b2_d = dt("b2_l", [8, C], F32, kind="ExternalInput").ap()
    lt_d = dt("lt_exc", [128, 128], BF16, kind="ExternalInput").ap()
    idf_d = dt("ident_f", [128, 128], F32, kind="ExternalInput").ap()
    oh16_d = dt("onehot16", [16, 16 * 128], BF16, kind="ExternalInput").ap()
    fold8_d = dt("fold8", [64, 8], F32, kind="ExternalInput").ap()
    rep8_d = dt("rep8", [8, 64], F32, kind="ExternalInput").ap()
    km_d = dt("killmask", [1, 256], F32, kind="ExternalInput").ap()
    onesrb_d = dt("ones_rowb", [1, 512], BF16, kind="ExternalInput").ap()

    w1bf = dt("w1bf", [128, E * 32768], BF16).ap()
    b1bf = dt("b1bf", [1, E * DI], BF16).ap()
    w2bf = dt("w2bf", [128, E * 32768], BF16).ap()
    ln2_h = dt("ln2_h", [N, C], BF16).ap()
    xin_hs = [dt(f"xin_h{e}", [CAP, C], BF16).ap() for e in range(E)]
    zz_d = dt("zz", [CAP, C], BF16, kind="ExternalInput").ap()
    ctb_hs = [dt(f"ctb_h{p}", [CAP + 4, C], BF16).ap() for p in range(2)]
    sc_loc = dt("sc_loc", [E, N], F32).ap()
    sc_ag = dt("sc_ag", [NCORES * E, N], F32).ap()
    out_d = dt("out", [N, C], F32, kind="ExternalOutput").ap()
    dbg = {}
    if debug:
        for nm, shp in [("d_kv", [128, 2560]), ("d_xmid", [N, C]), ("d_sctm", [128, 256]),
                        ("d_tau", [8, 2]), ("d_pos", [128, 256]), ("d_att", [128, 4096]),
                        ("d_ln2", [N, C]), ("d_ctb0", [CAP + 4, C]), ("d_ctb1", [CAP + 4, C]),
                        ("d_xin0", [CAP, C]), ("d_xin1", [CAP, C]),
                        ("d_p1", [128, 16384]), ("d_kvt", [128, 20480])]:
            dbg[nm] = dt(nm, shp, F32, kind="ExternalOutput").ap()

    st = nc.sbuf_tensor
    ctx = contextlib.ExitStack()

    AB_ELS = 74752
    arena = ctx.enter_context(st("arena", [128, AB_ELS], BF16))
    A = arena.ap()
    cur = [0]

    def aslice(els):
        o = cur[0]
        cur[0] += els
        assert cur[0] <= AB_ELS, f"arena overflow {cur[0]}"
        return o

    def ar(off, shape):
        tot = 1
        for d in shape:
            tot *= d
        apv = A[:, off:off + tot]
        if len(shape) > 1:
            letters = "abcdefg"[: len(shape)]
            spec = "p (" + " ".join(letters) + ") -> p " + " ".join(letters)
            apv = apv.rearrange(spec, **{l: d for l, d in zip(letters, shape)})
        return apv

    o_ln1t = aslice(4096)
    o_ln1f = aslice(4096)
    o_qk = aslice(4096)
    o_vfm = aslice(4096)
    o_vaug = aslice(4352)
    o_kq = aslice(8192)
    o_attn = aslice(4096)
    o_ln2t = aslice(4096)
    o_kvbf = aslice(2560)
    o_wq = aslice(16384)
    o_wo = aslice(8192)
    o_projp = aslice(4096)
    o_vts = aslice(4096)
    o_cs = o_vfm
    o_cs2 = o_vfm + 2560
    o_bd = o_vaug
    o_corr = o_vaug + 512
    attn_top = cur[0]

    cur[0] = 0
    f_w1q = aslice(2 * 4096)
    f_w2 = aslice(32768)
    f_xinT = aslice(10240)
    f_h = aslice(16384)
    f_ctb = aslice(2048)
    f_stag = aslice(3 * 1024)
    f_ln2b = aslice(2048)
    f_cnt8 = f_h
    ffn_top = cur[0]
    assert max(attn_top, ffn_top) <= AB_ELS, (attn_top, ffn_top)
    cur[0] = AB_ELS

    xt = ctx.enter_context(st("xt", [128, 4096], F32))
    kvsb = ctx.enter_context(st("kvsb", [128, 2560], F32))
    sctm = ctx.enter_context(st("sctm", [128, 256], F32))
    msk = ctx.enter_context(st("msk", [128, 256], F32))
    mskb = ctx.enter_context(st("mskb", [128, 256], BF16))
    gte = ctx.enter_context(st("gte", [128, 256], F32))
    offsc = ctx.enter_context(st("offsc", [128, 256], U32))
    offgu = ctx.enter_context(st("offgu", [128, 256], U32))
    offf = ctx.enter_context(st("offf", [128, 256], F32))
    lns = ctx.enter_context(st("lns", [128, 8], F32))
    lnA = ctx.enter_context(st("lnA", [128, 1], F32))
    lnB = ctx.enter_context(st("lnB", [128, 1], F32))
    lnMU = ctx.enter_context(st("lnMU", [128, 1], F32))
    lnT = ctx.enter_context(st("lnT", [128, 1], F32))
    lnVp = ctx.enter_context(st("lnVp", [128, 1], F32))
    lnLNV = ctx.enter_context(st("lnLNV", [128, 1], F32))
    lnRSTD = ctx.enter_context(st("lnRSTD", [128, 1], F32))
    smMX = ctx.enter_context(st("smMX", [128, 4], F32))
    smSUM = ctx.enter_context(st("smSUM", [128, 4], F32))
    smRCP = ctx.enter_context(st("smRCP", [128, 4], F32))
    mx8 = ctx.enter_context(st("mx8", [128, 16], F32))
    mxqf = ctx.enter_context(st("mxqf", [16, 512], BF16))
    dens = ctx.enter_context(st("dens", [16, 512], F32))
    rcp = ctx.enter_context(st("rcp", [16, 512], F32))
    rcpb = ctx.enter_context(st("rcpb", [16, 512], BF16))
    bisv = ctx.enter_context(st("bisv", [8, 8], F32))
    cntv = ctx.enter_context(st("cntv", [64, 1], F32))
    bsum = ctx.enter_context(st("bsum", [1, 256], F32))
    binc = ctx.enter_context(st("binc", [1, 256], F32))
    kmv = ctx.enter_context(st("kmv", [1, 256], F32))
    lt_s = ctx.enter_context(st("lt_s", [128, 128], BF16))
    idf_s = ctx.enter_context(st("idf_s", [128, 128], F32))
    oh16 = ctx.enter_context(st("oh16", [16, 2048], BF16))
    fold8 = ctx.enter_context(st("fold8s", [64, 8], F32))
    rep8 = ctx.enter_context(st("rep8s", [8, 64], F32))
    onesrb = ctx.enter_context(st("onesrbs", [1, 512], BF16))
    epsr = ctx.enter_context(st("epsrs", [1, 512], BF16))
    zrow = ctx.enter_context(st("zrows", [1, 512], BF16))
    bqkv = ctx.enter_context(st("bqkvs", [1, 2 * C], BF16))
    bout = ctx.enter_context(st("bouts", [1, C], BF16))
    b1q = ctx.enter_context(st("b1qs", [1, 1024], BF16))
    b2s = ctx.enter_context(st("b2ss", [8, C], BF16))
    wrs = ctx.enter_context(st("wrss", [128, 8 * E], BF16))
    onescol_bf = ctx.enter_context(st("onescolb", [128, 1], BF16))
    ones8s = ctx.enter_context(st("ones8s", [8, 128], F32))

    ps2 = ctx.enter_context(nc.psum_tensor("ps2", [128, 2048], F32))
    ps1k = ctx.enter_context(nc.psum_tensor("ps1k", [128, 1024], F32))
    psA = ctx.enter_context(nc.psum_tensor("psA", [128, 512], F32))
    psB = ctx.enter_context(nc.psum_tensor("psB", [128, 512], F32))

    P = Plan(nc, ctx)
    op, cnt = P.op, P.cnt

    def kvcol(pair):
        return (pair // 7) * 512 + (pair % 7) * 68

    op("gpsimd", lambda g: g.dma_start(out=ar(o_wq, [16384]),
                                       in_=wqkv_d[:, 8 * 1024:24 * 1024]), inc="dG")
    op("gpsimd", lambda g: g.dma_start(out=ar(o_wo, [8192]), in_=wout_d), inc="dG")
    op("gpsimd", lambda g: g.dma_start(out=ar(o_projp, [4096]), in_=projp_d), inc="dG")
    op("gpsimd", lambda g: g.dma_start(out=wrs[:, :], in_=wr_d), inc="dG")
    op("gpsimd", lambda g: g.dma_start(out=bqkv[:, :], in_=bqkv_d[:, C:3 * C]), inc="dG")
    op("gpsimd", lambda g: g.dma_start(out=bout[:, :], in_=bout_d), inc="dG")
    ph0_g = op("gpsimd", lambda g: g.dma_start(out=b2s[:, :], in_=b2_d), inc="dG")
    ph0_s = 0
    for dst, src in [(lt_s, lt_d), (idf_s, idf_d), (oh16, oh16_d), (fold8, fold8_d),
                     (rep8, rep8_d), (kmv, km_d), (onesrb, onesrb_d)]:
        ph0_s = op("sync", lambda s, d=dst, sr=src: s.dma_start(out=d[:, :], in_=sr), inc="dS")

    def ph0v(v):
        v.memset(kvsb[:, 0:508], 0.0)
        v.memset(kvsb[:, 509:2560], 0.0)
        v.memset(kvsb[:, 508:509], LNEPS)
        v.memset(epsr[:, :], EPS)
        v.memset(zrow[:, :], 0.0)
        va = ar(o_vaug, [4, 16, 68])
        v.memset(va, 0.0)
        return v.memset(va[:, :, :, 64:65], 1.0)
    ph0_v = op("vector", ph0v, inc="eV")
    op("scalar", lambda s: load_act_set(s, 6))

    for par in range(2):
        for r in range(4):
            for hf in range(2):
                op("gpsimd", lambda g, par=par, r=r, hf=hf: g.dma_start(
                    out=ctb_hs[par][CAP + r:CAP + r + 1, hf * 512:(hf + 1) * 512],
                    in_=zrow[:, :]), wait=[("eV", ph0_v)], inc="dG")

    def emit_ln(dst_off, fm_dst, qeng="sync", qsem="dS"):
        for sub in range(NSUB):
            xs = xt[:, sub * 1024:(sub + 1) * 1024]

            def vstats(v, xs=xs):
                v.reduce_sum(lnA[:, :], xs, axis=AX.X)
                v.tensor_tensor(ps2[:, 1024:2048], xs, xs, op=OP.mult)
                v.reduce_sum(lnB[:, :], ps2[:, 1024:2048], axis=AX.X)
                v.tensor_scalar(lnMU[:, :], lnA[:, :], 1.0 / C, None, op0=OP.mult)
                return v.tensor_tensor(lnT[:, :], lnA[:, :], lnA[:, :], op=OP.mult)
            op("vector", vstats, inc="eV")
            op("vector", lambda v: v.tensor_scalar(
                lnVp[:, :], lnT[:, :], -1.0 / (C * C), LNEPS, op0=OP.mult, op1=OP.add),
                wait=["eV"], inc="eV")

            op("scalar", lambda s: s.activation(lnLNV[:, :], lnB[:, :], AF.Ln,
                                                bias=lnVp[:, :], scale=1.0 / C),
               wait=["eV"], inc="eS")
            op("scalar", lambda s: s.activation(lnRSTD[:, :], lnLNV[:, :], AF.Exp,
                                                bias=0.0, scale=-0.5),
               wait=["eS"], inc="eS")

            def vapp(v, xs=xs, sub=sub):
                return v.tensor_scalar(ar(dst_off, [8, 4, 128])[:, :, sub, :], xs,
                                       lnMU[:, :], lnRSTD[:, :], op0=OP.subtract, op1=OP.mult)
            op("vector", vapp, wait=["eS"], inc="eV")
        ev = cnt["eV"]
        src = ar(dst_off, [8, 4, 128])
        dstf = ar(fm_dst, [8, 4, 128])
        ds = None
        for ci in range(8):
            ds = op(qeng, lambda s, ci=ci: s.dma_start_transpose(
                dstf[:, ci, :, :], src[:, ci, :, :]), wait=[("eV", ev)], inc=qsem)
        return ev, ds

    P.tag = "ph1"
    lf = ar(o_ln1f, [8, 512])
    wq = ar(o_wq, [16, 8, 128])
    pp = ar(o_projp, [16, 2, 128])
    va = ar(o_vaug, [4, 16, 68])
    kv_evac = 0
    prev_ln_ev = 0
    for t in range(NT):
        if t == 0:
            lw = ["eV"]
        elif debug:
            lw = ["eV", "dG"]
        else:
            lw = [("eV", prev_ln_ev)]
        ldx = op("sync", lambda s, t=t: s.dma_start(
            out=xt.ap().rearrange("p (a f) -> p a f", a=4),
            in_=x_d[t * TILE:(t + 1) * TILE, :]
            .rearrange("(a p) f -> p a f", p=128)),
            wait=lw, inc="dS")
        op("vector", lambda v: v.nop(), wait=[("dS", ldx)])
        ev_ln, ds_tr = emit_ln(o_ln1t, o_ln1f, "scalar", "dP")
        prev_ln_ev = ev_ln

        ev_evac = {}
        for j in range(16):
            ps = psA if (j % 2 == 0) else psB

            def tmm(tn, j=j, ps=ps):
                for ci in range(8):
                    tn.matmul(ps[:, :], wq[:, j, ci, :], lf[:, ci, :],
                              start=(ci == 0), stop=False)
                return tn.matmul(ps[:, :], bqkv[0:1, j * 128:(j + 1) * 128],
                                 onesrb[0:1, :], start=False, stop=True)
            wts = [("dP", ds_tr)]
            if t == 0 and j == 0:
                wts += [("dG", ph0_g), ("eV", ph0_v), ("dS", ph0_s)]
            if j >= 2:
                wts.append(("eS", ev_evac[j - 2]))
            et = op("tensor", tmm, wait=wts, inc="eT")

            def sev(s, j=j, ps=ps):
                dst = ar(o_qk, [8, 512]) if j < 8 else ar(o_vfm, [8, 512])
                if j < 8:
                    return s.mul(dst[:, j % 8, :], ps[:, :], 64.0 ** -0.25)
                return s.copy(dst[:, j % 8, :], ps[:, :])
            ev_evac[j] = op("scalar", sev, wait=[("eT", et)], inc="eS")
        evk = cnt["eS"]

        vf = ar(o_vfm, [8, 4, 128])
        vts3 = ar(o_vts, [8, 4, 128])
        for jc in range(8):
            op("sync", lambda s, jc=jc: s.dma_start_transpose(
                vts3[:, jc, :, :], vf[:, jc, :, :]),
                wait=[("eS", evk), ("eV", cnt["eV"])], inc="dS")
        dtr = cnt["dS"]
        va_ev = 0
        for sub in range(NSUB):
            def vcp(v, sub=sub):
                src = ar(o_vts, [8, 4, 2, 64])[:, :, sub, :, :]
                dst = va[:, sub, :, 0:64].rearrange("p (a b) d -> p a b d", a=8)
                return v.tensor_copy(dst, src)
            va_ev = op("vector", vcp, wait=[("dS", dtr)], inc="eV")

        kq = ar(o_kq, [2, 16, 2, 128])
        half_guard = [0, 0]
        for sub in range(NSUB):
            rot = sub % 2
            kp_ev = 0
            for r in range(4):
                half = (r % 2) * 1024

                def txp(tn, sub=sub, r=r, half=half):
                    last = None
                    for hh in range(4):
                        h = r * 4 + hh
                        last = tn.matmul(ps2[:, half + hh * 256: half + (hh + 1) * 256],
                                         ar(o_qk, [8, 4, 128])[:, h // 2, sub, :],
                                         pp[:, h, :, :].rearrange("p a b -> p (a b)"),
                                         start=True, stop=True)
                    return last
                wts = [("eS", evk), ("eS", half_guard[r % 2]), ("eV", kv_evac)]
                et = op("tensor", txp, wait=wts, inc="eT")

                def vmx(v, r=r, half=half):
                    return v.reduce_max(mx8[:, r * 4:(r + 1) * 4],
                                        ps2[:, half:half + 1024]
                                        .rearrange("p (h m) -> p h m", h=4),
                                        axis=AX.X, negate=True)
                ev = op("vector", vmx, wait=[("eT", et)], inc="eV")

                def sexp(s, r=r, half=half, rot=rot):
                    last = None
                    for hh in range(4):
                        h = r * 4 + hh
                        last = s.activation(
                            kq[:, rot, h, :, :].rearrange("p a b -> p (a b)"),
                            ps2[:, half + hh * 256: half + (hh + 1) * 256],
                            AF.Exp, bias=mx8[:, r * 4 + hh:r * 4 + hh + 1], scale=1.0)
                    return last
                es = op("scalar", sexp, wait=[("eV", ev)], inc="eS")
                half_guard[r % 2] = es

                def veps(v, r=r, rot=rot):
                    sl = kq[:, rot, r * 4:(r + 1) * 4, :, :].rearrange("p a b c -> p (a b c)")
                    return v.tensor_scalar(sl, sl, EPS, None, op0=OP.add)
                kp_ev = op("vector", veps, wait=[("eS", es)], inc="eV")

            def tkv(tn, sub=sub, rot=rot):
                last = None
                for pair in range(32):
                    h, pi = pair // 2, pair % 2
                    col = kvcol(pair)
                    if col < 2048:
                        dst = ps2[:, col:col + 65]
                    else:
                        dst = psA[:, col - 2048:col - 2048 + 65]
                    last = tn.matmul(dst, kq[:, rot, h, pi, :], va[:, sub, h, 0:65],
                                     start=True, stop=True)
                return last
            et = op("tensor", tkv, wait=[("eV", kp_ev), ("eV", va_ev),
                                         ("eV", kv_evac), ("eS", evk)], inc="eT")

            def vkv(v):
                last = None
                for b in range(5):
                    w_ = 476 if b < 4 else 272
                    src = ps2[:, b * 512:b * 512 + w_] if b < 4 else psA[:, 0:w_]
                    last = v.tensor_tensor(kvsb[:, b * 512:b * 512 + w_],
                                           kvsb[:, b * 512:b * 512 + w_], src, op=OP.add)
                return last
            kv_evac = op("vector", vkv, wait=[("eT", et)], inc="eV")
            half_guard = [0, 0]

        if debug and t == 1:
            dmp = 0
            for k_, off_ in enumerate([o_ln1t, o_qk]):
                dmp = op("gpsimd", lambda g, k_=k_, off_=off_: g.dma_start(
                    out=dbg["d_p1"][:, k_ * 4096:(k_ + 1) * 4096],
                    in_=A[:, off_:off_ + 4096]),
                    wait=[("eV", kv_evac)], inc="dG")
            op("gpsimd", lambda g: g.dma_start(
                out=dbg["d_p1"][:, 2 * 4096:3 * 4096], in_=xt[:, :]),
                wait=[("eV", kv_evac)], inc="dG")
            for k_, src_ in enumerate([lnMU, lnVp, lnLNV, lnRSTD]):
                op("gpsimd", lambda g, k_=k_, src_=src_: g.dma_start(
                    out=dbg["d_p1"][:, 3 * 4096 + k_:3 * 4096 + k_ + 1],
                    in_=src_[:, :], allow_slow_non_contiguous=True),
                    wait=[("eV", kv_evac)], inc="dG")
        if debug:
            op("gpsimd", lambda g, t=t: g.dma_start(
                out=dbg["d_kvt"][:, t * 2560:(t + 1) * 2560], in_=kvsb[:, :]),
                wait=[("eV", kv_evac)], inc="dG")

    P.tag = "wcast"
    for e in range(E):
        op("gpsimd", lambda g, e=e: g.dma_start(out=xin_hs[e], in_=zz_d), inc="dG")
    zfill_dG = cnt["dG"]
    op("gpsimd", lambda g: g.dma_start(out=b1bf, in_=b1_d), inc="dG")
    for e in range(E):
        op("gpsimd", lambda g, e=e: g.dma_start(
            out=w1bf[:, e * 32768:(e + 1) * 32768],
            in_=w1_d[:, e * 32768:(e + 1) * 32768]), inc="dG")
        op("gpsimd", lambda g, e=e: g.dma_start(
            out=w2bf[:, e * 32768:(e + 1) * 32768],
            in_=w2_d[:, e * 32768:(e + 1) * 32768]), inc="dG")
    wcast = cnt["dG"]

    P.tag = "ph1b"
    op("gpsimd", lambda g: g.dma_start(
        out=ar(o_wq, [16, 8, 128])[:, 0:8, :, :].rearrange("p a b c -> p (a b c)"),
        in_=wqkv_d[:, 0:8 * 1024]), wait=[("eT", cnt["eT"])], inc="dG")
    qload = op("gpsimd", lambda g: g.dma_start(
        out=bqkv[0:1, 0:C], in_=bqkv_d[0:1, 0:C]), inc="dG")

    kvbf = ar(o_kvbf, [2560])
    e1 = op("vector", lambda v: v.tensor_copy(kvbf, kvsb[:, :]),
            wait=[("eV", kv_evac)], inc="eV")

    def vbd(v):
        last = None
        bd = ar(o_bd, [32, 16])
        kb = ar(o_kvbf, [2560])
        v.memset(ar(o_bd, [512]), 0.0)
        for pair in range(32):
            h = pair // 2
            last = v.tensor_copy(bd[:, pair, h:h + 1], kb[:, kvcol(pair) + 64:kvcol(pair) + 65])
        return last
    bd_ev = op("vector", vbd, wait=[("eV", e1)], inc="eV")

    def tcs(tn):
        last = None
        kb = ar(o_kvbf, [2560])
        for pair in range(32):
            col = kvcol(pair)
            dst = ps2[0:1, col:col + 65] if col < 2048 else psB[0:1, col - 2048:col - 2048 + 65]
            last = tn.matmul(dst, onescol_bf[:, :], kb[:, col:col + 65], start=True, stop=True)
        return last
    op("vector", lambda v: v.memset(onescol_bf[:, :], 1.0), inc="eV")
    cs_et = op("tensor", tcs, wait=[("eV", cnt["eV"])], inc="eT")

    def scs(s):
        cs = ar(o_cs, [2560])[0:1, :]
        s.copy(cs[:, 0:2048], ps2[0:1, 0:2048])
        return s.copy(cs[:, 2048:2560], psB[0:1, 0:512])
    cs_es = op("scalar", scs, wait=[("eT", cs_et)], inc="eS")

    def vcs2(v):
        flat = ar(o_cs, [2560])[0:1, :]
        cs2f = ar(o_cs2, [1040])[0:1, :]
        last = None
        for h in range(16):
            c0, c1 = kvcol(2 * h), kvcol(2 * h + 1)
            v.tensor_tensor(cs2f[:, h * 64:(h + 1) * 64], flat[:, c0:c0 + 64],
                            flat[:, c1:c1 + 64], op=OP.add)
            last = v.tensor_tensor(cs2f[:, 1024 + h:1025 + h], flat[:, c0 + 64:c0 + 65],
                                   flat[:, c1 + 64:c1 + 65], op=OP.add)
        return last
    cs2_ev = op("vector", vcs2, wait=[("eS", cs_es)], inc="eV")

    if debug:
        op("sync", lambda s: s.dma_start(out=dbg["d_kv"], in_=kvsb[:, :]),
           wait=[("eV", kv_evac)], inc="dS")

    if stop_after == "ph1":
        op("gpsimd", lambda g: g.nop(), wait=[("dG", cnt["dG"]), ("dS", cnt["dS"]),
                                              ("eV", cnt["eV"]), ("eS", cnt["eS"]), ("eT", cnt["eT"])])
        op("sync", lambda s: s.dma_start(out=out_d[0:128, :], in_=xt[:, 0:1024]), inc="dS")
        op("sync", lambda s: s.nop(), wait=[("dS", cnt["dS"])])
        with nc.Block() as block:
            P.replay(block)
        ctx.close()
        return nc

    P.tag = "ph2"
    wo = ar(o_wo, [8, 8, 128])
    at = ar(o_attn, [8, 4, 128])
    kq2 = ar(o_kq, [4, 512])
    kvb5 = ar(o_kvbf, [2560])
    bd = ar(o_bd, [32, 16])
    cs2f = ar(o_cs2, [1040])[0:1, :]
    res_ev = 0
    prev_ev_ln2 = 0
    prev_st_x = 0
    for t in range(NT):
        lw2 = ([("eV", cnt["eV"]), ("dS", cnt["dS"])] if t == 0
               else [("eV", prev_ev_ln2), ("dS", prev_st_x)])
        ldx = op("sync", lambda s, t=t: s.dma_start(
            out=xt.ap().rearrange("p (a f) -> p a f", a=4),
            in_=x_d[t * TILE:(t + 1) * TILE, :]
            .rearrange("(a p) f -> p a f", p=128)),
            wait=lw2, inc="dS")
        op("vector", lambda v: v.nop(), wait=[("dS", ldx)])
        ev_ln, ds_tr = emit_ln(o_ln1t, o_ln1f, "scalar", "dP")

        ev_evac = {}
        for j in range(8):
            ps = psA if (j % 2 == 0) else psB

            def tmm(tn, j=j, ps=ps):
                for ci in range(8):
                    tn.matmul(ps[:, :], wq[:, j, ci, :], lf[:, ci, :],
                              start=(ci == 0), stop=False)
                return tn.matmul(ps[:, :], bqkv[0:1, j * 128:(j + 1) * 128],
                                 onesrb[0:1, :], start=False, stop=True)
            wts = [("dP", ds_tr)]
            if t == 0 and j == 0:
                wts.append(("dG", qload))
            if j >= 2:
                wts.append(("eS", ev_evac[j - 2]))
            et = op("tensor", tmm, wait=wts, inc="eT")
            ev_evac[j] = op("scalar", lambda s, j=j, ps=ps: s.mul(
                ar(o_qk, [8, 512])[:, j, :], ps[:, :], 64.0 ** -0.25),
                wait=[("eT", et)], inc="eS")
        evq = cnt["eS"]

        mx_done_es = evq
        for sub in range(NSUB):
            last_mx_ev = 0
            for r in range(4):
                half = (r % 2) * 1024

                def txp(tn, sub=sub, r=r, half=half):
                    last = None
                    for hh in range(4):
                        h = r * 4 + hh
                        last = tn.matmul(ps2[:, half + hh * 256: half + (hh + 1) * 256],
                                         ar(o_qk, [8, 4, 128])[:, h // 2, sub, :],
                                         pp[:, h, :, :].rearrange("p a b -> p (a b)"),
                                         start=True, stop=True)
                    return last
                wts = [("eS", evq), ("eV", cnt["eV"])]
                et = op("tensor", txp, wait=wts, inc="eT")

                def vmx(v, r=r, half=half):
                    return v.reduce_max(mx8[:, r * 4:(r + 1) * 4],
                                        ps2[:, half:half + 1024]
                                        .rearrange("p (h m) -> p h m", h=4),
                                        axis=AX.X, negate=True)
                last_mx_ev = op("vector", vmx, wait=[("eT", et)], inc="eV")

            et = op("tensor", lambda tn: tn.transpose(psB[0:16, 0:128], mx8[:, 0:16], idf_s[:, :]),
                    wait=[("eV", last_mx_ev), ("eS", mx_done_es)], inc="eT")
            mx_done_es = op("scalar", lambda s, sub=sub: s.copy(
                mxqf[:, sub * 128:(sub + 1) * 128], psB[0:16, 0:128]),
                wait=[("eT", et)], inc="eS")

        qk_all = ar(o_qk, [8, 512])
        num_guard = {0: 0, 1: 0}
        xq_guard = {}
        ev_eps = 0
        attn3 = ar(o_attn, [8, 512])
        XQS = [psA[:, 0:512], ps2[:, 1536:2048], ps2[:, 512:1024]]
        for pair in range(32):
            h, pi = pair // 2, pair % 2
            rot = pair % 4
            sl3 = pair % 3
            hb = (h % 2) * 64

            def txq(tn, h=h, pi=pi, sl3=sl3):
                xq = XQS[sl3]
                tn.matmul(xq, pp[:, h, pi, :], qk_all[:, h // 2, :], start=True, stop=False)
                return tn.matmul(xq, oh16[:, h * 128:(h + 1) * 128], mxqf[:, :],
                                 start=False, stop=True)
            wts = [("eS", mx_done_es), ("eS", xq_guard.get(pair - 3, 0))]
            et = op("tensor", txq, wait=wts, inc="eT")

            def sexp(s, rot=rot, sl3=sl3):
                return s.activation(kq2[:, rot, :], XQS[sl3], AF.Exp, bias=0.0, scale=1.0)
            es = op("scalar", sexp, wait=[("eT", et)], inc="eS")
            xq_guard[pair] = es
            ev_eps = op("vector", lambda v, rot=rot: v.tensor_scalar(
                kq2[:, rot, :], kq2[:, rot, :], EPS, None, op0=OP.add),
                wait=[("eS", es)], inc="eV")

            def tnum(tn, pair=pair, h=h, pi=pi, rot=rot, hb=hb):
                col = kvcol(pair)
                tp = None if hb == 0 else (0, 64)
                nps = psB[hb:hb + 64, 0:512] if h % 2 == 0 else ps2[hb:hb + 64, 0:512]
                tn.matmul(nps, kvb5[:, col:col + 64], kq2[:, rot, :],
                          start=(pi == 0), stop=False, tile_position=tp)
                last = tn.matmul(ps2[0:16, 1024:1536], bd[:, pair, :], kq2[:, rot, :],
                                 start=(pair == 0), stop=False)
                if pi == 1:
                    last = tn.matmul(nps, cs2f[:, h * 64:(h + 1) * 64], epsr[0:1, :],
                                     start=False, stop=True, tile_position=tp)
                return last
            wn = [("eV", ev_eps), ("eV", cs2_ev), ("eS", num_guard[h % 2])]
            et2 = op("tensor", tnum, wait=wn, inc="eT")

            if pi == 1:
                def sevn(s, h=h, hb=hb):
                    src = psB[hb:hb + 64, 0:512] if h % 2 == 0 else ps2[hb:hb + 64, 0:512]
                    return s.copy(attn3[hb:hb + 64, h // 2, :], src)
                num_guard[h % 2] = op("scalar", sevn, wait=[("eT", et2)], inc="eS")
        num_es = cnt["eS"]

        def tdeps(tn):
            return tn.matmul(ps2[0:16, 1024:1536], cs2f[:, 1024:1040], epsr[0:1, :],
                             start=False, stop=True)
        et = op("tensor", tdeps, wait=[], inc="eT")

        def vden(v):
            v.tensor_scalar(dens[:, :], ps2[0:16, 1024:1536], EPSD, None, op0=OP.add)
            v.reciprocal(rcp[:, :], dens[:, :])
            return v.tensor_copy(rcpb[:, :], rcp[:, :])
        ev_r = op("vector", vden, wait=[("eT", et), ("dS", cnt["dS"])], inc="eV")

        attn_ev = 0
        for h in range(16):
            et = op("tensor", lambda tn, h=h: tn.matmul(
                psA[:, 0:512], oh16[:, h * 128:(h + 1) * 128], rcpb[:, :],
                start=True, stop=True),
                wait=[("eV", ev_r), ("eS", num_es), ("eV", attn_ev)], inc="eT")

            def vmul(v, h=h):
                base = (h % 2) * 64
                dst = attn3[base:base + 64, h // 2, :]
                return v.tensor_tensor(dst, dst, psA[base:base + 64, 0:512], op=OP.mult)
            attn_ev = op("vector", vmul, wait=[("eT", et)], inc="eV")

        for sub in range(NSUB):
            def tout(tn, sub=sub):
                last = None
                for half in range(2):
                    for ci in range(8):
                        tn.matmul(ps1k[:, half * 512:(half + 1) * 512],
                                  at[:, ci, sub, :],
                                  wo[:, ci, :, :].rearrange("p a b -> p (a b)")
                                  [:, half * 512:(half + 1) * 512],
                                  start=(ci == 0), stop=False)
                    last = tn.matmul(ps1k[:, half * 512:(half + 1) * 512],
                                     onesrb[0:1, 0:128],
                                     bout[0:1, half * 512:(half + 1) * 512],
                                     start=False, stop=True)
                return last
            et = op("tensor", tout, wait=[("eV", attn_ev), ("eV", res_ev)], inc="eT")

            def vres(v, sub=sub):
                xs = xt[:, sub * 1024:(sub + 1) * 1024]
                return v.tensor_tensor(xs, xs, ps1k[:, :], op=OP.add)
            res_ev = op("vector", vres, wait=[("eT", et)], inc="eV")

        ev_ln2, ds_tr2 = emit_ln(o_ln2t, o_ln1f)
        st_x = op("sync", lambda s, t=t: s.dma_start(
            out=out_d[t * TILE:(t + 1) * TILE, :].rearrange("(a p) f -> p a f", p=128),
            in_=xt.ap().rearrange("p (a f) -> p a f", a=4)), wait=[("eV", ev_ln2)], inc="dS")
        prev_ev_ln2 = ev_ln2
        prev_st_x = st_x
        for ci in range(8):
            st_l = op("sync", lambda s, t=t, ci=ci: s.dma_start(
                out=ln2_h[t * TILE:(t + 1) * TILE, :]
                .rearrange("(a p) (ci i) -> p a ci i", p=128, ci=8)[:, :, ci, :],
                in_=ar(o_ln2t, [8, 4, 128])[:, ci, :, :]),
                wait=[("eV", ev_ln2)], inc="dS")

        def tsc(tn):
            last = None
            for ci in range(8):
                last = tn.matmul(psB[0:8, 0:512], wrs[:, ci * 8:(ci + 1) * 8],
                                 ar(o_ln1f, [8, 4, 128])[:, ci, :, :]
                                 .rearrange("p a b -> p (a b)"),
                                 start=(ci == 0), stop=(ci == 7))
            return last
        et_l = op("tensor", tsc, wait=[("dS", ds_tr2), ("eS", num_es),
                                       ("eV", cnt["eV"])], inc="eT")
        es_l = op("scalar", lambda s: s.copy(rcp[0:8, 0:512], psB[0:8, 0:512]),
                  wait=[("eT", et_l)], inc="eS")

        def tsp(tn):
            last = None
            for sub in range(NSUB):
                last = tn.transpose(psA[0:128, sub * 8:(sub + 1) * 8],
                                    rcp[0:8, sub * 128:(sub + 1) * 128], idf_s[0:8, 0:8])
            return last
        et2 = op("tensor", tsp, wait=[("eS", es_l)], inc="eT")
        ev2 = op("vector", lambda v: v.reduce_max(
            smMX[:, :], psA[0:128, 0:32].rearrange("p (a b) -> p a b", a=4),
            axis=AX.X, negate=True), wait=[("eT", et2)], inc="eV")

        def ssf(s, t=t):
            last = None
            for sub in range(NSUB):
                last = s.activation(
                    sctm[:, (t * NSUB + sub) * 8:(t * NSUB + sub + 1) * 8],
                    psA[0:128, sub * 8:(sub + 1) * 8],
                    AF.Exp, bias=smMX[:, sub:sub + 1], scale=1.0)
            return last
        es2 = op("scalar", ssf, wait=[("eV", ev2)], inc="eS")
        op("vector", lambda v, t=t: v.reduce_sum(
            smSUM[:, :], sctm[:, t * 32:(t + 1) * 32].rearrange("p (a b) -> p a b", a=4),
            axis=AX.X), wait=[("eS", es2)], inc="eV")
        op("vector", lambda v: v.reciprocal(smRCP[:, :], smSUM[:, :]),
           wait=["eV"], inc="eV")

        def vsf(v, t=t):
            last = None
            for sub in range(NSUB):
                sl = sctm[:, (t * NSUB + sub) * 8:(t * NSUB + sub + 1) * 8]
                last = v.tensor_scalar(sl, sl, smRCP[:, sub:sub + 1], None, op0=OP.mult)
            return last
        ev3 = op("vector", vsf, wait=["eV"], inc="eV")

        def tsb(tn, t=t):
            last = None
            for sub in range(NSUB):
                last = tn.transpose(psA[0:8, sub * 128:(sub + 1) * 128],
                                    sctm[:, (t * NSUB + sub) * 8:(t * NSUB + sub + 1) * 8],
                                    idf_s[:, :])
            return last
        et3 = op("tensor", tsb, wait=[("eV", ev3)], inc="eT")
        sc_guard = op("scalar", lambda s: s.copy(rcp[0:8, 0:512], psA[0:8, 0:512]),
                      wait=[("eT", et3)], inc="eS")
        st_s = op("sync", lambda s, t=t: s.dma_start(
            out=sc_loc[:, t * TILE:(t + 1) * TILE], in_=rcp[0:8, 0:512]),
            wait=[("eS", sc_guard)], inc="dS")

    xmid_ds = cnt["dS"]
    if debug:
        op("sync", lambda s: s.dma_start(out=dbg["d_att"], in_=xt[:, :]),
           wait=[("eV", res_ev)], inc="dS")
        op("sync", lambda s: s.dma_start(out=dbg["d_sctm"], in_=sctm[:, :]),
           wait=[("eS", cnt["eS"])], inc="dS")

    if stop_after == "ph2":
        op("gpsimd", lambda g: g.nop(), wait=[("dG", cnt["dG"]), ("dS", cnt["dS"]),
                                              ("eV", cnt["eV"]), ("eS", cnt["eS"]), ("eT", cnt["eT"])])
        with nc.Block() as block:
            P.replay(block)
        ctx.close()
        return nc

    P.tag = "ph3"
    agc = op("gpsimd", lambda g: g.collective_compute(
        "AllGather", OP.bypass, replica_groups=[list(range(NCORES))],
        ins=[sc_loc], outs=[sc_ag]), wait=[("dS", cnt["dS"])], inc="cc", amt=1)
    sc_all = xt.ap()[0:64, :]
    ld_sc = op("sync", lambda s: s.dma_start(out=sc_all, in_=sc_ag),
               wait=[("cc", agc)], inc="dS")

    def vbis0(v):
        v.memset(bisv[:, 0:1], 0.0)
        return v.memset(bisv[:, 1:2], 2.0)
    ev = op("vector", vbis0, wait=[("dS", ld_sc)], inc="eV")
    cnt8 = ar(f_cnt8, [4096])[0:64, :]
    P.tag = "ph3bis"
    for it in range(NBIS):
        ev = op("vector", lambda v: v.tensor_scalar(
            bisv[:, 2:3], bisv[:, 0:1], bisv[:, 1:2], 0.5, op0=OP.add, op1=OP.mult),
            wait=[("eV", ev), ("eT", cnt["eT"])], inc="eV")
        et = op("tensor", lambda tn: tn.matmul(psB[0:64, 0:1], rep8[:, :], bisv[:, 2:3],
                                               start=True, stop=True),
                wait=[("eV", ev)], inc="eT")

        def vcnt(v):
            return v.tensor_scalar(cnt8, sc_all, psB[0:64, 0:1], 0.0,
                                   op0=OP.is_ge, op1=OP.add, accum_out=cntv[:, :])
        ev = op("vector", vcnt, wait=[("eT", et)], inc="eV")
        et = op("tensor", lambda tn: tn.matmul(psA[0:8, 0:1], fold8[:, :], cntv[:, :],
                                               start=True, stop=True),
                wait=[("eV", ev)], inc="eT")

        def vupdA(v):
            v.tensor_scalar(bisv[:, 3:4], psA[0:8, 0:1], float(KTOP), None, op0=OP.is_ge)
            v.tensor_tensor(bisv[:, 4:5], bisv[:, 2:3], bisv[:, 0:1], op=OP.subtract)
            return v.tensor_tensor(bisv[:, 6:7], bisv[:, 1:2], bisv[:, 2:3], op=OP.subtract)
        ev = op("vector", vupdA, wait=[("eT", et)], inc="eV")

        def vupdB(v):
            v.tensor_tensor(bisv[:, 5:6], bisv[:, 3:4], bisv[:, 4:5], op=OP.mult)
            return v.tensor_tensor(bisv[:, 7:8], bisv[:, 3:4], bisv[:, 6:7], op=OP.mult)
        ev = op("vector", vupdB, wait=[("eV", ev)], inc="eV")

        def vupdC(v):
            v.tensor_tensor(bisv[:, 0:1], bisv[:, 0:1], bisv[:, 5:6], op=OP.add)
            return v.tensor_tensor(bisv[:, 1:2], bisv[:, 2:3], bisv[:, 7:8], op=OP.add)
        ev = op("vector", vupdC, wait=[("eV", ev)], inc="eV")
    bis_ev = ev
    P.tag = "ph3"
    if debug:
        op("sync", lambda s: s.dma_start(out=dbg["d_tau"], in_=bisv[:, 0:2]),
           wait=[("eV", bis_ev)], inc="dS")

    ev = op("vector", lambda v: v.tensor_scalar(lns[0:8, 0:8], idf_s[0:8, 0:8], bisv[:, 0:1],
                                                None, op0=OP.mult),
            wait=[("eV", bis_ev)], inc="eV")
    op("vector", lambda v: v.memset(ones8s[:, :], 1.0), inc="eV")
    et = op("tensor", lambda tn: tn.matmul(psB[0:128, 0:8], ones8s[:, :], lns[0:8, 0:8],
                                           start=True, stop=True),
            wait=[("eV", cnt["eV"])], inc="eT")

    def vmask(v):
        last = None
        for e in range(E):
            last = v.tensor_scalar(msk.ap().rearrange("p (b e) -> p b e", e=8)[:, :, e],
                                   sctm.ap().rearrange("p (b e) -> p b e", e=8)[:, :, e],
                                   psB[0:128, e:e + 1], None, op0=OP.is_ge)
        return last
    ev = op("vector", vmask, wait=[("eT", et)], inc="eV")

    def vmg(v):
        v.tensor_copy(mskb[:, :], msk[:, :])
        return v.tensor_tensor(gte[:, :], sctm[:, :], msk[:, :], op=OP.mult)
    ev_g = op("vector", vmg, wait=[("eV", ev)], inc="eV")

    def tpos(tn):
        tn.matmul(ps1k[:, 0:256], lt_s[:, :], mskb[:, :], start=True, stop=True)
        return tn.matmul(psA[0:1, 0:256], onescol_bf[:, :], mskb[:, :], start=True, stop=True)
    et = op("tensor", tpos, wait=[("eV", ev_g)], inc="eT")
    es = op("scalar", lambda s: s.copy(bsum[:, :], psA[0:1, 0:256]),
            wait=[("eT", et)], inc="eS")

    ev = op("vector", lambda v: v.tensor_copy(
        binc.ap().rearrange("x (e b) -> x e b", b=32),
        bsum.ap().rearrange("x (b e) -> x e b", e=8)), wait=[("eS", es)], inc="eV")
    ev = op("vector", lambda v: v.tensor_tensor_scan(
        bsum[:, :], kmv[:, :], binc[:, :], 0.0, op0=OP.mult, op1=OP.add),
        wait=[("eV", ev)], inc="eV")
    ev = op("vector", lambda v: v.tensor_tensor(
        bsum[:, :], bsum[:, :], binc[:, :], op=OP.subtract),
        wait=[("eV", ev)], inc="eV")

    et = op("tensor", lambda tn: tn.matmul(
        psB[:, 0:256], ones8s[0:1, 0:128],
        bsum.ap().rearrange("x (e b) -> x b e", b=32),
        start=True, stop=True),
        wait=[("eV", cnt["eV"])], inc="eT")

    if debug:
        dbgpos_hold = ctx.enter_context(st("dbgpos", [128, 256], F32))
    evo = op("vector", lambda v: v.tensor_scalar(
        offf[:, :], msk[:, :], 1.0, -1e6, op0=OP.subtract, op1=OP.mult),
        wait=[("eT", et)], inc="eV")
    evo = op("vector", lambda v: v.tensor_tensor(
        offf[:, :], offf[:, :], ps1k[:, 0:256], op=OP.add), wait=[("eV", evo)], inc="eV")
    evo = op("vector", lambda v: v.tensor_tensor(
        offf[:, :], offf[:, :], psB[:, 0:256], op=OP.add), wait=[("eV", evo)], inc="eV")
    evo = op("vector", lambda v: v.tensor_copy(offsc[:, :], offf[:, :]),
             wait=[("eV", evo)], inc="eV")
    evo = op("vector", lambda v: v.tensor_scalar(
        offf[:, :], offf[:, :], float(CAP), None, op0=OP.min), wait=[("eV", evo)], inc="eV")
    evo = op("vector", lambda v: v.tensor_copy(offgu[:, :], offf[:, :]),
             wait=[("eV", evo)], inc="eV")
    if debug:
        def vdbgp(v):
            v.tensor_copy(dbgpos_hold[:, :], ps1k[:, 0:256])
            return v.tensor_tensor(dbgpos_hold[:, :], dbgpos_hold[:, :],
                                   psB[:, 0:256], op=OP.add)
        evo = op("vector", vdbgp, wait=[("eV", evo)], inc="eV")
    ev_off = evo
    if debug:
        op("sync", lambda s: s.dma_start(out=dbg["d_pos"], in_=dbgpos_hold[:, :]),
           wait=[("eV", ev_off)], inc="dS")

    corr_ds = 0
    for blk in range(NBLK):
        et = op("tensor", lambda tn, blk=blk: tn.transpose(
            psB[0:8, 0:128], gte.ap().rearrange("p (b e) -> p b e", e=8)[:, blk, :],
            idf_s[:, :]), wait=[("eV", ev_off), ("eS", cnt["eS"])], inc="eT")
        es = op("scalar", lambda s: s.copy(mxqf[0:8, 0:128], psB[0:8, 0:128]),
                wait=[("eT", et)], inc="eS")

        def tcor(tn):
            tn.matmul(ps1k[:, 0:512], mxqf[0:8, 0:128], b2s[:, 0:512], start=True, stop=True)
            return tn.matmul(ps1k[:, 512:1024], mxqf[0:8, 0:128], b2s[:, 512:1024],
                             start=True, stop=True)
        et2 = op("tensor", tcor, wait=[("eS", es)], inc="eT")
        es2 = op("scalar", lambda s: s.copy(ar(o_corr, [1024]), ps1k[:, :]),
                 wait=[("eT", et2)], inc="eS")
        corr_ds = op("gpsimd", lambda g, blk=blk: g.dma_start(
            out=out_d[blk * 128:(blk + 1) * 128, :], in_=ar(o_corr, [1024]),
            accum_op=OP.add), wait=[("eS", es2), ("dS", xmid_ds), ("dG", wcast)], inc="dG")

    if stop_after == "ph3":
        op("gpsimd", lambda g: g.nop(), wait=[("dG", cnt["dG"]), ("dS", cnt["dS"]),
                                              ("eV", cnt["eV"]), ("eS", cnt["eS"]), ("eT", cnt["eT"])])
        op("sync", lambda s: s.nop(), wait=[("dS", cnt["dS"]), ("dG", cnt["dG"])])
        with nc.Block() as block:
            P.replay(block)
        ctx.close()
        return nc

    P.tag = "ph4"
    w1q = ar(f_w1q, [2, 4, 8, 128])
    w2t = ar(f_w2, [32, 1024])
    xinT = ar(f_xinT, [8, 1280])
    ht = ar(f_h, [32, 512])
    ctb = ar(f_ctb, [2, 1024])
    stag = ar(f_stag, [3, 1024])
    ln2b = ar(f_ln2b, [2, 1024])

    bc_reg = {}

    def get_bc(g):
        if "r" not in bc_reg:
            bc_reg["r"] = g.to_reg(CAP - 1)
        return bc_reg["r"]

    CHUNKS = [(0, 512), (512, 1024), (1024, 1280)]
    w1q_free = [0, 0]
    xin_free = 0
    h_free_eT = 0
    gather_done_dG = {0: 0, 1: 0}
    sc_done = {}
    evac_es_prev = 0
    prev_w2_phase_eT = 0
    gelu_hist = {}
    accum_prev = 0
    op("scalar", lambda s: s.preload_act_table(AF.Gelu), wait=[("eS", cnt["eS"])])

    P.tag = "ph4scat"
    sc_done_blk = {}
    for blk in range(NBLK):
        par = blk % 2
        ldb = op("sync", lambda s, blk=blk, par=par: s.dma_start(
            out=ln2b[:, par, :], in_=ln2_h[blk * 128:(blk + 1) * 128, :]),
            wait=[("dG", sc_done_blk.get(blk - 2, 0))], inc="dS")
        for e in range(E):
            def gsc(g, e=e, blk=blk, par=par):
                return g.indirect_dma_start(
                    out=xin_hs[e],
                    out_offset=IndirectOffsetOnAxis(
                        offsc.ap().rearrange("p (b q) -> p b q", q=8)[:, blk, e:e + 1], 0),
                    in_=ln2b[:, par, :], in_offset=None,
                    bounds_check=get_bc(g), oob_is_err=False)
            op("gpsimd", gsc, wait=[("dS", ldb), ("eV", ev_off)], inc="dG")
        sc_done_blk[blk] = cnt["dG"]
    scat_all = cnt["dG"]

    for e in range(E):
        P.tag = "ph4w2ld"
        w2_ds = op("scalar", lambda s, e=e: s.dma_start(
            out=ar(f_w2, [32768]), in_=w2bf[:, e * 32768:(e + 1) * 32768]),
            wait=[("eT", prev_w2_phase_eT), ("dG", corr_ds if e == 0 else wcast)], inc="dP")

        P.tag = "ph4xpose"
        trl = 0
        for ci in range(8):
            trl = op("sync", lambda s, e=e, ci=ci: s.dma_start_transpose(
                xinT[:, ci, :], xin_hs[e][:, ci * 128:(ci + 1) * 128]),
                wait=[("dG", scat_all), ("eT", xin_free)], inc="dS")
        tr_done = trl

        w1_ds = {}

        def plan_w1_load(k):
            ic_, q_ = k // 8, k % 8
            slot_ = k % 2
            oldtag, P.tag = P.tag, "ph4w1ld"
            op("scalar", lambda s, e=e, q_=q_, slot_=slot_: s.dma_start(
                out=w1q[:, slot_, :, :, :].rearrange("p a b c -> p (a b c)"),
                in_=w1bf[:, e * 32768 + q_ * 4096:e * 32768 + (q_ + 1) * 4096]),
                wait=[("eT", w1q_free[slot_]), ("dG", wcast)], inc="dP")
            w1_ds[(ic_, q_)] = op("scalar", lambda s, e=e, q_=q_, slot_=slot_: s.dma_start(
                out=b1q[0:1, slot_ * 512:(slot_ + 1) * 512],
                in_=b1bf[0:1, e * DI + q_ * 512:e * DI + (q_ + 1) * 512]),
                inc="dP")
            P.tag = oldtag
        plan_w1_load(0)
        plan_w1_load(1)

        gelu_es = 0
        P.tag = "ph4mm"
        for ic, (c0, c1) in enumerate(CHUNKS):
            nch = c1 - c0
            for j in range(32):
                q = j // 4
                slot = (ic * 8 + q) % 2

                def tw1(tn, e=e, j=j, slot=slot, c0=c0, c1=c1, nch=nch):
                    ps = psA if (j % 2 == 0) else psB
                    for ci in range(8):
                        tn.matmul(ps[:, 0:nch], w1q[:, slot, j % 4, ci, :],
                                  xinT[:, ci, c0:c1], start=(ci == 0), stop=False)
                    return tn.matmul(ps[:, 0:nch],
                                     b1q[0:1, slot * 512 + (j % 4) * 128:slot * 512 + (j % 4 + 1) * 128],
                                     onesrb[0:1, 0:nch], start=False, stop=True)
                wts = [("dP", w1_ds[(ic, q)]), ("dS", tr_done),
                       ("eS", gelu_hist.get((e, ic, j - 2), 0)),
                       ("eT", h_free_eT)]
                et = op("tensor", tw1, wait=wts, inc="eT")
                w1q_free[slot] = et
                if j % 4 == 3:
                    knext = ic * 8 + q + 2
                    if knext < 24:
                        plan_w1_load(knext)

                def sg(s, j=j, nch=nch):
                    ps = psA if (j % 2 == 0) else psB
                    return s.activation(ht[:, j, 0:nch], ps[:, 0:nch], AF.Gelu,
                                        bias=0.0, scale=1.0)
                gelu_es = op("scalar", sg, wait=[("eT", et)], inc="eS")
                gelu_hist[(e, ic, j)] = gelu_es

            for sb in range(c0 // 128, c1 // 128):
                par2 = sb % 2

                def tw2(tn, sb=sb, c0=c0, par2=par2):
                    pps = ps1k[:, 0:1024] if par2 == 0 else ps2[:, 0:1024]
                    last = None
                    for j in range(32):
                        for half in range(2):
                            last = tn.matmul(pps[:, half * 512:(half + 1) * 512],
                                             ht[:, j, sb * 128 - c0:sb * 128 - c0 + 128],
                                             w2t[:, j, half * 512:(half + 1) * 512],
                                             start=(j == 0), stop=(j == 31))
                    return last
                et = op("tensor", tw2, wait=[("eS", gelu_es), ("dP", w2_ds),
                                             ("eS", evac_es_prev)], inc="eT")
                if sb == c1 // 128 - 1:
                    h_free_eT = et

                def sev2(s, par2=par2):
                    pps = ps1k[:, 0:1024] if par2 == 0 else ps2[:, 0:1024]
                    return s.copy(ctb[:, par2, :], pps)
                es = op("scalar", sev2, wait=[("eT", et)], inc="eS")
                evac_es_prev = es
                op("sync", lambda s, e=e, sb=sb, par2=par2: s.dma_start(
                    out=ctb_hs[e % 2][sb * 128:(sb + 1) * 128, :], in_=ctb[:, par2, :]),
                   wait=[("eS", es), ("dG", gather_done_dG[e % 2])], inc="dS")
        ctb_ds = cnt["dS"]
        prev_w2_phase_eT = cnt["eT"]
        xin_free = cnt["eT"]

        P.tag = "ph4gath"
        gl = 0
        for blk in range(NBLK):
            sl = blk % 2

            def gga(g, e=e, blk=blk, sl=sl):
                return g.indirect_dma_start(
                    out=stag[:, sl, :], out_offset=None,
                    in_=ctb_hs[e % 2],
                    in_offset=IndirectOffsetOnAxis(
                        offgu.ap().rearrange("p (b q) -> p b q", q=8)[:, blk, e:e + 1], 0))
            dg = op("gpsimd", gga, wait=[("dS", ctb_ds), ("eV", cnt["eV"])], inc="dG")

            def vgt(v, e=e, blk=blk, sl=sl):
                return v.tensor_scalar(stag[:, 2, :], stag[:, sl, :],
                                       gte.ap().rearrange("p (b q) -> p b q", q=8)[:, blk, e:e + 1],
                                       None, op0=OP.mult)
            ev2 = op("vector", vgt, wait=[("dG", dg), ("dG", accum_prev)], inc="eV")
            accum_prev = op("gpsimd", lambda g, blk=blk: g.dma_start(
                out=out_d[blk * 128:(blk + 1) * 128, :], in_=stag[:, 2, :],
                accum_op=OP.add), wait=[("eV", ev2)], inc="dG")
            gl = accum_prev
        gather_done_dG[e % 2] = gl

    if debug:
        op("gpsimd", lambda g: g.dma_start(out=dbg["d_ln2"], in_=ln2_h),
           wait=[("dG", cnt["dG"]), ("dS", cnt["dS"])], inc="dG")
        op("gpsimd", lambda g: g.dma_start(out=dbg["d_ctb0"], in_=ctb_hs[0]), inc="dG")
        op("gpsimd", lambda g: g.dma_start(out=dbg["d_ctb1"], in_=ctb_hs[1]), inc="dG")
        op("gpsimd", lambda g: g.dma_start(out=dbg["d_xin0"], in_=xin_hs[0]), inc="dG")
        op("gpsimd", lambda g: g.dma_start(out=dbg["d_xin1"], in_=xin_hs[1]), inc="dG")

    op("gpsimd", lambda g: g.nop(), wait=[("dG", cnt["dG"]), ("dS", cnt["dS"])])
    op("sync", lambda s: s.nop(), wait=[("dG", cnt["dG"])])

    with nc.Block() as block:
        P.replay(block)
    ctx.close()
    return nc


def _prep_inputs(L0_attn_x, L0_attn_w_qkv, L0_attn_b_qkv, L0_attn_w_out, L0_attn_b_out,
                 L0_attn_proj_matrix, L0_ffn_Wr, L0_ffn_W1, L0_ffn_b1, L0_ffn_W2, L0_ffn_b2):
    f32 = np.float32
    wqkv = np.ascontiguousarray(L0_attn_w_qkv, dtype=f32)
    wq_l = wqkv.reshape(8, 128, 24, 128).transpose(1, 2, 0, 3).reshape(128, 24 * 8 * 128)
    wout = np.ascontiguousarray(L0_attn_w_out, dtype=f32)
    wo_l = wout.reshape(8, 128, 8, 128).transpose(1, 0, 2, 3).reshape(128, 8 * 8 * 128)
    proj = np.asarray(L0_attn_proj_matrix, dtype=f32)
    pp = np.zeros((128, H, 2, 128), f32)
    for h in range(H):
        base = (h % 2) * 64
        for pi in range(2):
            pp[base:base + 64, h, pi, :] = proj[h, pi::2, :].T
    projp = pp.reshape(128, H * 2 * 128)
    wr = np.ascontiguousarray(L0_ffn_Wr, dtype=f32)
    wr_l = wr.reshape(8, 128, 8).transpose(1, 0, 2).reshape(128, 64)
    W1 = np.asarray(L0_ffn_W1, dtype=f32)
    w1_l = W1.reshape(8, 8, 128, 4, 8, 128).transpose(2, 0, 3, 4, 1, 5).reshape(128, E * 32768)
    W2 = np.asarray(L0_ffn_W2, dtype=f32)
    w2_l = W2.reshape(8, 32, 128, 1024).transpose(2, 0, 1, 3).reshape(128, E * 32768)
    b1_l = np.asarray(L0_ffn_b1, dtype=f32).reshape(1, E * DI)
    b2_l = np.ascontiguousarray(L0_ffn_b2, dtype=f32)
    bqkv = np.asarray(L0_attn_b_qkv, dtype=f32).reshape(1, 3 * C)
    bout = np.asarray(L0_attn_b_out, dtype=f32).reshape(1, C)

    lt = np.ascontiguousarray(np.tril(np.ones((128, 128), f32), -1).T)
    lt_bf = lt.astype(ml_dtypes.bfloat16)
    idf = np.eye(128, dtype=f32)
    oh = np.zeros((16, 16 * 128), ml_dtypes.bfloat16)
    for h in range(16):
        oh[h, h * 128:(h + 1) * 128] = 1.0
    fold = np.zeros((64, 8), f32)
    for p in range(64):
        fold[p, p % 8] = 1.0
    rp = np.zeros((8, 64), f32)
    for p in range(64):
        rp[p % 8, p] = 1.0
    km = np.ones((1, 256), f32)
    km[0, 0::32] = 0.0
    onesrb = np.ones((1, 512), ml_dtypes.bfloat16)

    shared = dict(wqkv_l=wq_l, bqkv=bqkv, wout_l=wo_l, bout=bout, projpad=projp,
                  wr_l=wr_l, w1_l=w1_l, b1_l=b1_l, w2_l=w2_l, b2_l=b2_l,
                  lt_exc=lt_bf, ident_f=idf, onehot16=oh, fold8=fold, rep8=rp,
                  killmask=km, ones_rowb=onesrb)
    x = np.asarray(L0_attn_x, dtype=f32)
    zz = np.zeros((1280, 1024), dtype=ml_dtypes.bfloat16)
    in_maps = []
    for c in range(NCORES):
        m = dict(shared)
        m["x"] = np.ascontiguousarray(x[c])
        m["zz"] = zz
        in_maps.append(m)
    return in_maps


_NC_CACHE = {}


def kernel(**inputs):
    in_maps = _prep_inputs(**inputs)
    if "nc" not in _NC_CACHE:
        _NC_CACHE["nc"] = build_nc(debug=False)
    nc = _NC_CACHE["nc"]
    res = run_bass_kernel_spmd(nc, in_maps, core_ids=list(range(NCORES)))
    out = np.stack([np.asarray(r["out"]) for r in res.results], axis=0)
    return out


# version 09
# speedup vs baseline: 1.0082x; 1.0082x over previous
import math
import contextlib
import numpy as np
import ml_dtypes
import concourse.bass as bass
import concourse.mybir as mybir
from concourse.bass import IndirectOffsetOnAxis
from concourse.bass_utils import run_bass_kernel_spmd

F32 = mybir.dt.float32
BF16 = mybir.dt.bfloat16
U32 = mybir.dt.uint32
OP = mybir.AluOpType
AF = mybir.ActivationFunctionType
AX = mybir.AxisListType

NCORES = 8
N = 4096
C = 1024
H = 16
E = 8
DI = 4096
TILE = 512
NT = N // TILE
NSUB = TILE // 128
NBLK = N // 128
CAP = 1280
NSB = CAP // 128
KTOP = 8192
NBIS = 38
EPS = 1e-6
EPSD = 256e-6
LNEPS = 1e-5

ENGS = ("tensor", "vector", "scalar", "gpsimd", "sync")


def load_act_set(s, idx):
    return s.add_instruction(mybir.InstLoadActFuncSet(
        name=s.bass.get_next_instruction_name(), act_func_set_id=idx, ins=[], outs=[]))


class Plan:
    def __init__(self, nc, ctx):
        self.nc = nc
        self.ops = {e: [] for e in ENGS}
        self.cnt = {}
        self.sems = {}
        self.tag = "ph0"
        for s in ("dS", "dG", "dP", "dV", "eT", "eV", "eS", "cc"):
            self.sems[s] = ctx.enter_context(nc.semaphore(s))
            self.cnt[s] = 0

    def op(self, eng, fn, wait=(), inc=None, amt=None):
        wt = []
        for item in wait:
            if isinstance(item, tuple):
                if item[1] > 0:
                    wt.append((item[0], item[1]))
            else:
                if self.cnt[item] > 0:
                    wt.append((item, self.cnt[item]))
        if inc is not None:
            if amt is None:
                amt = 16 if inc.startswith("d") else 1
            self.cnt[inc] += amt
        self.ops[eng].append((fn, wt, inc, amt, self.tag))
        return self.cnt[inc] if inc else None

    def replay(self, block):
        sems = self.sems
        for ename in ENGS:
            oplist = self.ops[ename]

            def body(e, oplist=oplist, ename=ename):
                for fn, wt, inc, amt, tag in oplist:
                    for s, v in wt:
                        e.wait_ge(sems[s], v)
                    with self.nc.named_scope(f"{tag}.{ename}"):
                        instr = fn(e)
                    if inc is not None:
                        instr.then_inc(sems[inc], amt)
            getattr(block, ename)(body)


def build_nc(debug=False, stop_after=None):
    nc = bass.Bass("TRN2", target_bir_lowering=False, detect_race_conditions=False)
    dt = nc.dram_tensor

    x_d = dt("x", [N, C], F32, kind="ExternalInput").ap()
    wqkv_d = dt("wqkv_l", [128, 24 * 8 * 128], F32, kind="ExternalInput").ap()
    bqkv_d = dt("bqkv", [1, 3 * C], F32, kind="ExternalInput").ap()
    wout_d = dt("wout_l", [128, 8 * 8 * 128], F32, kind="ExternalInput").ap()
    bout_d = dt("bout", [1, C], F32, kind="ExternalInput").ap()
    projp_d = dt("projpad", [128, H * 2 * 128], F32, kind="ExternalInput").ap()
    wr_d = dt("wr_l", [128, 8 * E], F32, kind="ExternalInput").ap()
    w1_d = dt("w1_l", [128, E * 32768], F32, kind="ExternalInput").ap()
    b1_d = dt("b1_l", [128, E * 32], F32, kind="ExternalInput").ap()
    w2_d = dt("w2_l", [128, E * 32768], F32, kind="ExternalInput").ap()
    b2_d = dt("b2_l", [8, C], F32, kind="ExternalInput").ap()
    lt_d = dt("lt_exc", [128, 128], BF16, kind="ExternalInput").ap()
    idf_d = dt("ident_f", [128, 128], F32, kind="ExternalInput").ap()
    oh16_d = dt("onehot16", [16, 16 * 128], BF16, kind="ExternalInput").ap()
    fold8_d = dt("fold8", [64, 8], F32, kind="ExternalInput").ap()
    rep8_d = dt("rep8", [8, 64], F32, kind="ExternalInput").ap()
    km_d = dt("killmask", [1, 256], F32, kind="ExternalInput").ap()
    onesrb_d = dt("ones_rowb", [1, 512], BF16, kind="ExternalInput").ap()

    w1bf = dt("w1bf", [128, E * 32768], BF16).ap()
    w2bf = dt("w2bf", [128, E * 32768], BF16).ap()
    ln2_h = dt("ln2_h", [N, C], BF16).ap()
    xin_hs = [dt(f"xin_h{e}", [CAP, C], BF16).ap() for e in range(E)]
    ctb_hs = [dt(f"ctb_h{p}", [CAP + 4, C], BF16).ap() for p in range(2)]
    sc_loc = dt("sc_loc", [E, N], F32).ap()
    sc_ag = dt("sc_ag", [NCORES * E, N], F32).ap()
    out_d = dt("out", [N, C], F32, kind="ExternalOutput").ap()
    dbg = {}
    if debug:
        for nm, shp in [("d_kv", [128, 2560]), ("d_xmid", [N, C]), ("d_sctm", [128, 256]),
                        ("d_tau", [8, 2]), ("d_pos", [128, 256]), ("d_att", [128, 4096]),
                        ("d_ln2", [N, C]), ("d_ctb0", [CAP + 4, C]), ("d_ctb1", [CAP + 4, C]),
                        ("d_xin0", [CAP, C]), ("d_xin1", [CAP, C]),
                        ("d_p1", [128, 16384]), ("d_kvt", [128, 20480])]:
            dbg[nm] = dt(nm, shp, F32, kind="ExternalOutput").ap()

    st = nc.sbuf_tensor
    ctx = contextlib.ExitStack()

    AB_ELS = 74752
    arena = ctx.enter_context(st("arena", [128, AB_ELS], BF16))
    A = arena.ap()
    cur = [0]

    def aslice(els):
        o = cur[0]
        cur[0] += els
        assert cur[0] <= AB_ELS, f"arena overflow {cur[0]}"
        return o

    def ar(off, shape):
        tot = 1
        for d in shape:
            tot *= d
        apv = A[:, off:off + tot]
        if len(shape) > 1:
            letters = "abcdefg"[: len(shape)]
            spec = "p (" + " ".join(letters) + ") -> p " + " ".join(letters)
            apv = apv.rearrange(spec, **{l: d for l, d in zip(letters, shape)})
        return apv

    o_ln1t = aslice(4096)
    o_ln1f = aslice(4096)
    o_qk = aslice(4096)
    o_vfm = aslice(4096)
    o_vaug = aslice(4352)
    o_kq = aslice(8192)
    o_attn = aslice(4096)
    o_ln2t = aslice(4096)
    o_kvbf = aslice(2560)
    o_wq = aslice(16384)
    o_wo = aslice(8192)
    o_projp = aslice(4096)
    o_vts = aslice(4096)
    o_cs = o_vfm
    o_cs2 = o_vfm + 2560
    o_bd = o_vaug
    o_corr = o_vaug + 512
    attn_top = cur[0]

    cur[0] = 0
    f_w1q = aslice(2 * 4096)
    f_w2 = aslice(32768)
    f_xinT = aslice(10240)
    f_h = aslice(16384)
    f_ctb = aslice(2048)
    f_stag = aslice(3 * 1024)
    f_ln2b = aslice(2048)
    f_cnt8 = f_h
    ffn_top = cur[0]
    assert max(attn_top, ffn_top) <= AB_ELS, (attn_top, ffn_top)
    cur[0] = AB_ELS

    xt = ctx.enter_context(st("xt", [128, 4096], F32))
    kvsb = ctx.enter_context(st("kvsb", [128, 2560], F32))
    sctm = ctx.enter_context(st("sctm", [128, 256], F32))
    msk = ctx.enter_context(st("msk", [128, 256], F32))
    mskb = ctx.enter_context(st("mskb", [128, 256], BF16))
    gte = ctx.enter_context(st("gte", [128, 256], F32))
    offsc = ctx.enter_context(st("offsc", [128, 256], U32))
    offgu = ctx.enter_context(st("offgu", [128, 256], U32))
    offf = ctx.enter_context(st("offf", [128, 256], F32))
    lns = ctx.enter_context(st("lns", [128, 8], F32))
    lnA = ctx.enter_context(st("lnA", [128, 1], F32))
    lnB = ctx.enter_context(st("lnB", [128, 1], F32))
    lnMU = ctx.enter_context(st("lnMU", [128, 1], F32))
    lnT = ctx.enter_context(st("lnT", [128, 1], F32))
    lnVp = ctx.enter_context(st("lnVp", [128, 1], F32))
    lnLNV = ctx.enter_context(st("lnLNV", [128, 1], F32))
    lnRSTD = ctx.enter_context(st("lnRSTD", [128, 1], F32))
    smMX = ctx.enter_context(st("smMX", [128, 1], F32))
    smSUM = ctx.enter_context(st("smSUM", [128, 1], F32))
    smRCP = ctx.enter_context(st("smRCP", [128, 1], F32))
    mx8 = ctx.enter_context(st("mx8", [128, 16], F32))
    mxqf = ctx.enter_context(st("mxqf", [16, 512], BF16))
    dens = ctx.enter_context(st("dens", [16, 512], F32))
    rcp = ctx.enter_context(st("rcp", [16, 512], F32))
    rcpb = ctx.enter_context(st("rcpb", [16, 512], BF16))
    bisv = ctx.enter_context(st("bisv", [8, 8], F32))
    cntv = ctx.enter_context(st("cntv", [64, 1], F32))
    bsum = ctx.enter_context(st("bsum", [1, 256], F32))
    binc = ctx.enter_context(st("binc", [1, 256], F32))
    kmv = ctx.enter_context(st("kmv", [1, 256], F32))
    lt_s = ctx.enter_context(st("lt_s", [128, 128], BF16))
    idf_s = ctx.enter_context(st("idf_s", [128, 128], F32))
    oh16 = ctx.enter_context(st("oh16", [16, 2048], BF16))
    fold8 = ctx.enter_context(st("fold8s", [64, 8], F32))
    rep8 = ctx.enter_context(st("rep8s", [8, 64], F32))
    onesrb = ctx.enter_context(st("onesrbs", [1, 512], BF16))
    epsr = ctx.enter_context(st("epsrs", [1, 512], BF16))
    zrow = ctx.enter_context(st("zrows", [1, 512], BF16))
    bqkv = ctx.enter_context(st("bqkvs", [1, 2 * C], BF16))
    bout = ctx.enter_context(st("bouts", [1, C], BF16))
    b1s = ctx.enter_context(st("b1ss", [128, 256], F32))
    b2s = ctx.enter_context(st("b2ss", [8, C], BF16))
    wrs = ctx.enter_context(st("wrss", [128, 8 * E], BF16))
    onescol_bf = ctx.enter_context(st("onescolb", [128, 1], BF16))
    ones8s = ctx.enter_context(st("ones8s", [8, 128], F32))

    ps2 = ctx.enter_context(nc.psum_tensor("ps2", [128, 2048], F32))
    ps1k = ctx.enter_context(nc.psum_tensor("ps1k", [128, 1024], F32))
    psA = ctx.enter_context(nc.psum_tensor("psA", [128, 512], F32))
    psB = ctx.enter_context(nc.psum_tensor("psB", [128, 512], F32))

    P = Plan(nc, ctx)
    op, cnt = P.op, P.cnt

    def kvcol(pair):
        return (pair // 7) * 512 + (pair % 7) * 68

    op("gpsimd", lambda g: g.dma_start(out=ar(o_wq, [16384]),
                                       in_=wqkv_d[:, 8 * 1024:24 * 1024]), inc="dG")
    op("gpsimd", lambda g: g.dma_start(out=ar(o_wo, [8192]), in_=wout_d), inc="dG")
    op("gpsimd", lambda g: g.dma_start(out=ar(o_projp, [4096]), in_=projp_d), inc="dG")
    op("gpsimd", lambda g: g.dma_start(out=wrs[:, :], in_=wr_d), inc="dG")
    op("gpsimd", lambda g: g.dma_start(out=bqkv[:, :], in_=bqkv_d[:, C:3 * C]), inc="dG")
    op("gpsimd", lambda g: g.dma_start(out=bout[:, :], in_=bout_d), inc="dG")
    ph0_g = op("gpsimd", lambda g: g.dma_start(out=b2s[:, :], in_=b2_d), inc="dG")
    ph0_s = 0
    for dst, src in [(lt_s, lt_d), (idf_s, idf_d), (oh16, oh16_d), (fold8, fold8_d),
                     (rep8, rep8_d), (kmv, km_d), (onesrb, onesrb_d)]:
        ph0_s = op("sync", lambda s, d=dst, sr=src: s.dma_start(out=d[:, :], in_=sr), inc="dS")

    def ph0v(v):
        v.memset(xt[:, 0:1024], 0.0)
        v.memset(kvsb[:, 0:508], 0.0)
        v.memset(kvsb[:, 509:2560], 0.0)
        v.memset(kvsb[:, 508:509], LNEPS)
        v.memset(epsr[:, :], EPS)
        v.memset(zrow[:, :], 0.0)
        va = ar(o_vaug, [4, 16, 68])
        v.memset(va, 0.0)
        return v.memset(va[:, :, :, 64:65], 1.0)
    ph0_v = op("vector", ph0v, inc="eV")
    op("scalar", lambda s: load_act_set(s, 6))

    for par in range(2):
        for r in range(4):
            for hf in range(2):
                op("gpsimd", lambda g, par=par, r=r, hf=hf: g.dma_start(
                    out=ctb_hs[par][CAP + r:CAP + r + 1, hf * 512:(hf + 1) * 512],
                    in_=zrow[:, :]), wait=[("eV", ph0_v)], inc="dG")
    op("sync", lambda s: s.dma_start(out=b1s[:, :], in_=b1_d), inc="dS")
    zf = 0
    for e in range(E):
        for sb in range(NSB):
            zf = op("gpsimd", lambda g, e=e, sb=sb: g.dma_start(
                out=xin_hs[e][sb * 128:(sb + 1) * 128, :], in_=xt[:, 0:1024]),
                wait=[("eV", ph0_v)], inc="dG")
    zfill_dG = zf
    for e in range(E):
        op("gpsimd", lambda g, e=e: g.dma_start(
            out=w1bf[:, e * 32768:(e + 1) * 32768],
            in_=w1_d[:, e * 32768:(e + 1) * 32768]), inc="dG")
        op("gpsimd", lambda g, e=e: g.dma_start(
            out=w2bf[:, e * 32768:(e + 1) * 32768],
            in_=w2_d[:, e * 32768:(e + 1) * 32768]), inc="dG")
    wcast = cnt["dG"]

    def emit_ln(dst_off, fm_dst):
        for sub in range(NSUB):
            xs = xt[:, sub * 1024:(sub + 1) * 1024]

            def vstats(v, xs=xs):
                v.reduce_sum(lnA[:, :], xs, axis=AX.X)
                v.tensor_tensor(ps2[:, 1024:2048], xs, xs, op=OP.mult)
                v.reduce_sum(lnB[:, :], ps2[:, 1024:2048], axis=AX.X)
                v.tensor_scalar(lnMU[:, :], lnA[:, :], 1.0 / C, None, op0=OP.mult)
                return v.tensor_tensor(lnT[:, :], lnA[:, :], lnA[:, :], op=OP.mult)
            op("vector", vstats, inc="eV")
            op("vector", lambda v: v.tensor_scalar(
                lnVp[:, :], lnT[:, :], -1.0 / (C * C), LNEPS, op0=OP.mult, op1=OP.add),
                wait=["eV"], inc="eV")

            op("scalar", lambda s: s.activation(lnLNV[:, :], lnB[:, :], AF.Ln,
                                                bias=lnVp[:, :], scale=1.0 / C),
               wait=["eV"], inc="eS")
            op("scalar", lambda s: s.activation(lnRSTD[:, :], lnLNV[:, :], AF.Exp,
                                                bias=0.0, scale=-0.5),
               wait=["eS"], inc="eS")

            def vapp(v, xs=xs, sub=sub):
                return v.tensor_scalar(ar(dst_off, [8, 4, 128])[:, :, sub, :], xs,
                                       lnMU[:, :], lnRSTD[:, :], op0=OP.subtract, op1=OP.mult)
            op("vector", vapp, wait=["eS"], inc="eV")
        ev = cnt["eV"]
        src = ar(dst_off, [8, 4, 128])
        dstf = ar(fm_dst, [8, 4, 128])
        ds = None
        for ci in range(8):
            ds = op("sync", lambda s, ci=ci: s.dma_start_transpose(
                dstf[:, ci, :, :], src[:, ci, :, :]), wait=[("eV", ev)], inc="dS")
        return ev, ds

    P.tag = "ph1"
    lf = ar(o_ln1f, [8, 512])
    wq = ar(o_wq, [16, 8, 128])
    pp = ar(o_projp, [16, 2, 128])
    va = ar(o_vaug, [4, 16, 68])
    kv_evac = 0
    prev_ln_ev = 0
    for t in range(NT):
        if t == 0:
            lw = ["eV", ("dG", zfill_dG)]
        elif debug:
            lw = ["eV", "dG"]
        else:
            lw = [("eV", prev_ln_ev)]
            if t == 1:
                lw.append(("dG", zfill_dG))
        ldx = op("sync", lambda s, t=t: s.dma_start(
            out=xt.ap().rearrange("p (a f) -> p a f", a=4),
            in_=x_d[t * TILE:(t + 1) * TILE, :]
            .rearrange("(a p) f -> p a f", p=128)),
            wait=lw, inc="dS")
        op("vector", lambda v: v.nop(), wait=[("dS", ldx)])
        ev_ln, ds_tr = emit_ln(o_ln1t, o_ln1f)
        prev_ln_ev = ev_ln

        ev_evac = {}
        for j in range(16):
            ps = psA if (j % 2 == 0) else psB

            def tmm(tn, j=j, ps=ps):
                for ci in range(8):
                    tn.matmul(ps[:, :], wq[:, j, ci, :], lf[:, ci, :],
                              start=(ci == 0), stop=False)
                return tn.matmul(ps[:, :], bqkv[0:1, j * 128:(j + 1) * 128],
                                 onesrb[0:1, :], start=False, stop=True)
            wts = [("dS", ds_tr)]
            if t == 0 and j == 0:
                wts += [("dG", ph0_g), ("eV", ph0_v), ("dS", ph0_s)]
            if j >= 2:
                wts.append(("eS", ev_evac[j - 2]))
            et = op("tensor", tmm, wait=wts, inc="eT")

            def sev(s, j=j, ps=ps):
                dst = ar(o_qk, [8, 512]) if j < 8 else ar(o_vfm, [8, 512])
                if j < 8:
                    return s.mul(dst[:, j % 8, :], ps[:, :], 64.0 ** -0.25)
                return s.copy(dst[:, j % 8, :], ps[:, :])
            ev_evac[j] = op("scalar", sev, wait=[("eT", et)], inc="eS")
        evk = cnt["eS"]

        vf = ar(o_vfm, [8, 4, 128])
        vts3 = ar(o_vts, [8, 4, 128])
        for jc in range(8):
            op("sync", lambda s, jc=jc: s.dma_start_transpose(
                vts3[:, jc, :, :], vf[:, jc, :, :]),
                wait=[("eS", evk), ("eV", cnt["eV"])], inc="dS")
        dtr = cnt["dS"]
        va_ev = 0
        for sub in range(NSUB):
            def vcp(v, sub=sub):
                src = ar(o_vts, [8, 4, 2, 64])[:, :, sub, :, :]
                dst = va[:, sub, :, 0:64].rearrange("p (a b) d -> p a b d", a=8)
                return v.tensor_copy(dst, src)
            va_ev = op("vector", vcp, wait=[("dS", dtr)], inc="eV")

        kq = ar(o_kq, [2, 16, 2, 128])
        half_guard = [0, 0]
        for sub in range(NSUB):
            rot = sub % 2
            kp_ev = 0
            for r in range(4):
                half = (r % 2) * 1024

                def txp(tn, sub=sub, r=r, half=half):
                    last = None
                    for hh in range(4):
                        h = r * 4 + hh
                        last = tn.matmul(ps2[:, half + hh * 256: half + (hh + 1) * 256],
                                         ar(o_qk, [8, 4, 128])[:, h // 2, sub, :],
                                         pp[:, h, :, :].rearrange("p a b -> p (a b)"),
                                         start=True, stop=True)
                    return last
                wts = [("eS", evk), ("eS", half_guard[r % 2]), ("eV", kv_evac)]
                et = op("tensor", txp, wait=wts, inc="eT")

                def vmx(v, r=r, half=half):
                    return v.reduce_max(mx8[:, r * 4:(r + 1) * 4],
                                        ps2[:, half:half + 1024]
                                        .rearrange("p (h m) -> p h m", h=4),
                                        axis=AX.X, negate=True)
                ev = op("vector", vmx, wait=[("eT", et)], inc="eV")

                def sexp(s, r=r, half=half, rot=rot):
                    last = None
                    for hh in range(4):
                        h = r * 4 + hh
                        last = s.activation(
                            kq[:, rot, h, :, :].rearrange("p a b -> p (a b)"),
                            ps2[:, half + hh * 256: half + (hh + 1) * 256],
                            AF.Exp, bias=mx8[:, r * 4 + hh:r * 4 + hh + 1], scale=1.0)
                    return last
                es = op("scalar", sexp, wait=[("eV", ev)], inc="eS")
                half_guard[r % 2] = es

                def veps(v, r=r, rot=rot):
                    sl = kq[:, rot, r * 4:(r + 1) * 4, :, :].rearrange("p a b c -> p (a b c)")
                    return v.tensor_scalar(sl, sl, EPS, None, op0=OP.add)
                kp_ev = op("vector", veps, wait=[("eS", es)], inc="eV")

            def tkv(tn, sub=sub, rot=rot):
                last = None
                for pair in range(32):
                    h, pi = pair // 2, pair % 2
                    col = kvcol(pair)
                    if col < 2048:
                        dst = ps2[:, col:col + 65]
                    else:
                        dst = psA[:, col - 2048:col - 2048 + 65]
                    last = tn.matmul(dst, kq[:, rot, h, pi, :], va[:, sub, h, 0:65],
                                     start=True, stop=True)
                return last
            et = op("tensor", tkv, wait=[("eV", kp_ev), ("eV", va_ev),
                                         ("eV", kv_evac), ("eS", evk)], inc="eT")

            def vkv(v):
                last = None
                for b in range(5):
                    w_ = 476 if b < 4 else 272
                    src = ps2[:, b * 512:b * 512 + w_] if b < 4 else psA[:, 0:w_]
                    last = v.tensor_tensor(kvsb[:, b * 512:b * 512 + w_],
                                           kvsb[:, b * 512:b * 512 + w_], src, op=OP.add)
                return last
            kv_evac = op("vector", vkv, wait=[("eT", et)], inc="eV")
            half_guard = [0, 0]

        if debug and t == 1:
            dmp = 0
            for k_, off_ in enumerate([o_ln1t, o_qk]):
                dmp = op("gpsimd", lambda g, k_=k_, off_=off_: g.dma_start(
                    out=dbg["d_p1"][:, k_ * 4096:(k_ + 1) * 4096],
                    in_=A[:, off_:off_ + 4096]),
                    wait=[("eV", kv_evac)], inc="dG")
            op("gpsimd", lambda g: g.dma_start(
                out=dbg["d_p1"][:, 2 * 4096:3 * 4096], in_=xt[:, :]),
                wait=[("eV", kv_evac)], inc="dG")
            for k_, src_ in enumerate([lnMU, lnVp, lnLNV, lnRSTD]):
                op("gpsimd", lambda g, k_=k_, src_=src_: g.dma_start(
                    out=dbg["d_p1"][:, 3 * 4096 + k_:3 * 4096 + k_ + 1],
                    in_=src_[:, :], allow_slow_non_contiguous=True),
                    wait=[("eV", kv_evac)], inc="dG")
        if debug:
            op("gpsimd", lambda g, t=t: g.dma_start(
                out=dbg["d_kvt"][:, t * 2560:(t + 1) * 2560], in_=kvsb[:, :]),
                wait=[("eV", kv_evac)], inc="dG")

    P.tag = "ph1b"
    op("gpsimd", lambda g: g.dma_start(
        out=ar(o_wq, [16, 8, 128])[:, 0:8, :, :].rearrange("p a b c -> p (a b c)"),
        in_=wqkv_d[:, 0:8 * 1024]), wait=[("eT", cnt["eT"])], inc="dG")
    qload = op("gpsimd", lambda g: g.dma_start(
        out=bqkv[0:1, 0:C], in_=bqkv_d[0:1, 0:C]), inc="dG")

    kvbf = ar(o_kvbf, [2560])
    e1 = op("vector", lambda v: v.tensor_copy(kvbf, kvsb[:, :]),
            wait=[("eV", kv_evac)], inc="eV")

    def vbd(v):
        last = None
        bd = ar(o_bd, [32, 16])
        kb = ar(o_kvbf, [2560])
        v.memset(ar(o_bd, [512]), 0.0)
        for pair in range(32):
            h = pair // 2
            last = v.tensor_copy(bd[:, pair, h:h + 1], kb[:, kvcol(pair) + 64:kvcol(pair) + 65])
        return last
    bd_ev = op("vector", vbd, wait=[("eV", e1)], inc="eV")

    def tcs(tn):
        last = None
        kb = ar(o_kvbf, [2560])
        for pair in range(32):
            col = kvcol(pair)
            dst = ps2[0:1, col:col + 65] if col < 2048 else psB[0:1, col - 2048:col - 2048 + 65]
            last = tn.matmul(dst, onescol_bf[:, :], kb[:, col:col + 65], start=True, stop=True)
        return last
    op("vector", lambda v: v.memset(onescol_bf[:, :], 1.0), inc="eV")
    cs_et = op("tensor", tcs, wait=[("eV", cnt["eV"])], inc="eT")

    def scs(s):
        cs = ar(o_cs, [2560])[0:1, :]
        s.copy(cs[:, 0:2048], ps2[0:1, 0:2048])
        return s.copy(cs[:, 2048:2560], psB[0:1, 0:512])
    cs_es = op("scalar", scs, wait=[("eT", cs_et)], inc="eS")

    def vcs2(v):
        flat = ar(o_cs, [2560])[0:1, :]
        cs2f = ar(o_cs2, [1040])[0:1, :]
        last = None
        for h in range(16):
            c0, c1 = kvcol(2 * h), kvcol(2 * h + 1)
            v.tensor_tensor(cs2f[:, h * 64:(h + 1) * 64], flat[:, c0:c0 + 64],
                            flat[:, c1:c1 + 64], op=OP.add)
            last = v.tensor_tensor(cs2f[:, 1024 + h:1025 + h], flat[:, c0 + 64:c0 + 65],
                                   flat[:, c1 + 64:c1 + 65], op=OP.add)
        return last
    cs2_ev = op("vector", vcs2, wait=[("eS", cs_es)], inc="eV")

    if debug:
        op("sync", lambda s: s.dma_start(out=dbg["d_kv"], in_=kvsb[:, :]),
           wait=[("eV", kv_evac)], inc="dS")

    if stop_after == "ph1":
        op("gpsimd", lambda g: g.nop(), wait=[("dG", cnt["dG"]), ("dS", cnt["dS"]),
                                              ("eV", cnt["eV"]), ("eS", cnt["eS"]), ("eT", cnt["eT"])])
        op("sync", lambda s: s.dma_start(out=out_d[0:128, :], in_=xt[:, 0:1024]), inc="dS")
        op("sync", lambda s: s.nop(), wait=[("dS", cnt["dS"])])
        with nc.Block() as block:
            P.replay(block)
        ctx.close()
        return nc

    P.tag = "ph2"
    wo = ar(o_wo, [8, 8, 128])
    at = ar(o_attn, [8, 4, 128])
    kq2 = ar(o_kq, [2, 512])
    kvb5 = ar(o_kvbf, [2560])
    bd = ar(o_bd, [32, 16])
    cs2f = ar(o_cs2, [1040])[0:1, :]
    res_ev = 0
    for t in range(NT):
        ldx = op("sync", lambda s, t=t: s.dma_start(
            out=xt.ap().rearrange("p (a f) -> p a f", a=4),
            in_=x_d[t * TILE:(t + 1) * TILE, :]
            .rearrange("(a p) f -> p a f", p=128)),
            wait=[("eV", cnt["eV"]), ("dS", cnt["dS"])], inc="dS")
        op("vector", lambda v: v.nop(), wait=[("dS", ldx)])
        ev_ln, ds_tr = emit_ln(o_ln1t, o_ln1f)

        ev_evac = {}
        for j in range(8):
            ps = psA if (j % 2 == 0) else psB

            def tmm(tn, j=j, ps=ps):
                for ci in range(8):
                    tn.matmul(ps[:, :], wq[:, j, ci, :], lf[:, ci, :],
                              start=(ci == 0), stop=False)
                return tn.matmul(ps[:, :], bqkv[0:1, j * 128:(j + 1) * 128],
                                 onesrb[0:1, :], start=False, stop=True)
            wts = [("dS", ds_tr)]
            if t == 0 and j == 0:
                wts.append(("dG", qload))
            if j >= 2:
                wts.append(("eS", ev_evac[j - 2]))
            et = op("tensor", tmm, wait=wts, inc="eT")
            ev_evac[j] = op("scalar", lambda s, j=j, ps=ps: s.mul(
                ar(o_qk, [8, 512])[:, j, :], ps[:, :], 64.0 ** -0.25),
                wait=[("eT", et)], inc="eS")
        evq = cnt["eS"]

        mx_done_es = evq
        for sub in range(NSUB):
            last_mx_ev = 0
            for r in range(4):
                half = (r % 2) * 1024

                def txp(tn, sub=sub, r=r, half=half):
                    last = None
                    for hh in range(4):
                        h = r * 4 + hh
                        last = tn.matmul(ps2[:, half + hh * 256: half + (hh + 1) * 256],
                                         ar(o_qk, [8, 4, 128])[:, h // 2, sub, :],
                                         pp[:, h, :, :].rearrange("p a b -> p (a b)"),
                                         start=True, stop=True)
                    return last
                wts = [("eS", evq), ("eV", cnt["eV"])]
                et = op("tensor", txp, wait=wts, inc="eT")

                def vmx(v, r=r, half=half):
                    return v.reduce_max(mx8[:, r * 4:(r + 1) * 4],
                                        ps2[:, half:half + 1024]
                                        .rearrange("p (h m) -> p h m", h=4),
                                        axis=AX.X, negate=True)
                last_mx_ev = op("vector", vmx, wait=[("eT", et)], inc="eV")

            et = op("tensor", lambda tn: tn.transpose(psB[0:16, 0:128], mx8[:, 0:16], idf_s[:, :]),
                    wait=[("eV", last_mx_ev), ("eS", mx_done_es)], inc="eT")
            mx_done_es = op("scalar", lambda s, sub=sub: s.copy(
                mxqf[:, sub * 128:(sub + 1) * 128], psB[0:16, 0:128]),
                wait=[("eT", et)], inc="eS")

        qk_all = ar(o_qk, [8, 512])
        num_guard = {0: 0, 1: 0}
        xq_guard = {}
        ev_eps = 0
        attn3 = ar(o_attn, [8, 512])
        for pair in range(32):
            h, pi = pair // 2, pair % 2
            rot = pair % 2
            hb = (h % 2) * 64

            def txq(tn, h=h, pi=pi, rot=rot):
                xq = psA[:, 0:512] if rot == 0 else ps2[:, 1536:2048]
                tn.matmul(xq, pp[:, h, pi, :], qk_all[:, h // 2, :], start=True, stop=False)
                return tn.matmul(xq, oh16[:, h * 128:(h + 1) * 128], mxqf[:, :],
                                 start=False, stop=True)
            wts = [("eS", mx_done_es), ("eS", xq_guard.get(pair - 2, 0))]
            et = op("tensor", txq, wait=wts, inc="eT")

            def sexp(s, rot=rot):
                xq = psA[:, 0:512] if rot == 0 else ps2[:, 1536:2048]
                return s.activation(kq2[:, rot, :], xq, AF.Exp, bias=0.0, scale=1.0)
            es = op("scalar", sexp, wait=[("eT", et)], inc="eS")
            xq_guard[pair] = es
            ev_eps = op("vector", lambda v, rot=rot: v.tensor_scalar(
                kq2[:, rot, :], kq2[:, rot, :], EPS, None, op0=OP.add),
                wait=[("eS", es)], inc="eV")

            def tnum(tn, pair=pair, h=h, pi=pi, rot=rot, hb=hb):
                col = kvcol(pair)
                tp = None if hb == 0 else (0, 64)
                nps = psB[hb:hb + 64, 0:512] if h % 2 == 0 else ps2[hb:hb + 64, 0:512]
                tn.matmul(nps, kvb5[:, col:col + 64], kq2[:, rot, :],
                          start=(pi == 0), stop=False, tile_position=tp)
                last = tn.matmul(ps2[0:16, 1024:1536], bd[:, pair, :], kq2[:, rot, :],
                                 start=(pair == 0), stop=False)
                if pi == 1:
                    last = tn.matmul(nps, cs2f[:, h * 64:(h + 1) * 64], epsr[0:1, :],
                                     start=False, stop=True, tile_position=tp)
                return last
            wn = [("eV", ev_eps), ("eV", cs2_ev), ("eS", num_guard[h % 2])]
            et2 = op("tensor", tnum, wait=wn, inc="eT")

            if pi == 1:
                def sevn(s, h=h, hb=hb):
                    src = psB[hb:hb + 64, 0:512] if h % 2 == 0 else ps2[hb:hb + 64, 0:512]
                    return s.copy(attn3[hb:hb + 64, h // 2, :], src)
                num_guard[h % 2] = op("scalar", sevn, wait=[("eT", et2)], inc="eS")
        num_es = cnt["eS"]

        def tdeps(tn):
            return tn.matmul(ps2[0:16, 1024:1536], cs2f[:, 1024:1040], epsr[0:1, :],
                             start=False, stop=True)
        et = op("tensor", tdeps, wait=[], inc="eT")

        def vden(v):
            v.tensor_scalar(dens[:, :], ps2[0:16, 1024:1536], EPSD, None, op0=OP.add)
            v.reciprocal(rcp[:, :], dens[:, :])
            return v.tensor_copy(rcpb[:, :], rcp[:, :])
        ev_r = op("vector", vden, wait=[("eT", et), ("dS", cnt["dS"])], inc="eV")

        attn_ev = 0
        for h in range(16):
            et = op("tensor", lambda tn, h=h: tn.matmul(
                psA[:, 0:512], oh16[:, h * 128:(h + 1) * 128], rcpb[:, :],
                start=True, stop=True),
                wait=[("eV", ev_r), ("eS", num_es), ("eV", attn_ev)], inc="eT")

            def vmul(v, h=h):
                base = (h % 2) * 64
                dst = attn3[base:base + 64, h // 2, :]
                return v.tensor_tensor(dst, dst, psA[base:base + 64, 0:512], op=OP.mult)
            attn_ev = op("vector", vmul, wait=[("eT", et)], inc="eV")

        for sub in range(NSUB):
            def tout(tn, sub=sub):
                last = None
                for half in range(2):
                    for ci in range(8):
                        tn.matmul(ps1k[:, half * 512:(half + 1) * 512],
                                  at[:, ci, sub, :],
                                  wo[:, ci, :, :].rearrange("p a b -> p (a b)")
                                  [:, half * 512:(half + 1) * 512],
                                  start=(ci == 0), stop=False)
                    last = tn.matmul(ps1k[:, half * 512:(half + 1) * 512],
                                     onesrb[0:1, 0:128],
                                     bout[0:1, half * 512:(half + 1) * 512],
                                     start=False, stop=True)
                return last
            et = op("tensor", tout, wait=[("eV", attn_ev), ("eV", res_ev)], inc="eT")

            def vres(v, sub=sub):
                xs = xt[:, sub * 1024:(sub + 1) * 1024]
                return v.tensor_tensor(xs, xs, ps1k[:, :], op=OP.add)
            res_ev = op("vector", vres, wait=[("eT", et)], inc="eV")

        ev_ln2, ds_tr2 = emit_ln(o_ln2t, o_ln1f)
        st_x = op("sync", lambda s, t=t: s.dma_start(
            out=out_d[t * TILE:(t + 1) * TILE, :].rearrange("(a p) f -> p a f", p=128),
            in_=xt.ap().rearrange("p (a f) -> p a f", a=4)), wait=[("eV", ev_ln2)], inc="dS")
        for ci in range(8):
            st_l = op("sync", lambda s, t=t, ci=ci: s.dma_start(
                out=ln2_h[t * TILE:(t + 1) * TILE, :]
                .rearrange("(a p) (ci i) -> p a ci i", p=128, ci=8)[:, :, ci, :],
                in_=ar(o_ln2t, [8, 4, 128])[:, ci, :, :]),
                wait=[("eV", ev_ln2)], inc="dS")

        def tsc(tn):
            last = None
            for ci in range(8):
                last = tn.matmul(psB[0:8, 0:512], wrs[:, ci * 8:(ci + 1) * 8],
                                 ar(o_ln1f, [8, 4, 128])[:, ci, :, :]
                                 .rearrange("p a b -> p (a b)"),
                                 start=(ci == 0), stop=(ci == 7))
            return last
        et_l = op("tensor", tsc, wait=[("dS", ds_tr2), ("eS", num_es),
                                       ("eV", cnt["eV"])], inc="eT")
        es_l = op("scalar", lambda s: s.copy(rcp[0:8, 0:512], psB[0:8, 0:512]),
                  wait=[("eT", et_l)], inc="eS")
        sc_guard = es_l
        for sub in range(NSUB):
            et2 = op("tensor", lambda tn, sub=sub: tn.transpose(
                psA[0:128, 0:8], rcp[0:8, sub * 128:(sub + 1) * 128], idf_s[0:8, 0:8]),
                wait=[("eS", sc_guard)], inc="eT")
            ev2 = op("vector", lambda v: v.reduce_max(smMX[:, :], psA[0:128, 0:8],
                                                      axis=AX.X, negate=True),
                     wait=[("eT", et2)], inc="eV")
            es2 = op("scalar", lambda s, t=t, sub=sub: s.activation(
                sctm[:, (t * NSUB + sub) * 8:(t * NSUB + sub + 1) * 8], psA[0:128, 0:8],
                AF.Exp, bias=smMX[:, :], scale=1.0),
                wait=[("eV", ev2)], inc="eS")

            op("vector", lambda v, t=t, sub=sub: v.reduce_sum(
                smSUM[:, :], sctm[:, (t * NSUB + sub) * 8:(t * NSUB + sub + 1) * 8],
                axis=AX.X), wait=[("eS", es2)], inc="eV")
            op("vector", lambda v: v.reciprocal(smRCP[:, :], smSUM[:, :]),
               wait=["eV"], inc="eV")
            ev3 = op("vector", lambda v, t=t, sub=sub: v.tensor_scalar(
                sctm[:, (t * NSUB + sub) * 8:(t * NSUB + sub + 1) * 8],
                sctm[:, (t * NSUB + sub) * 8:(t * NSUB + sub + 1) * 8],
                smRCP[:, :], None, op0=OP.mult), wait=["eV"], inc="eV")
            et3 = op("tensor", lambda tn, t=t, sub=sub: tn.transpose(
                psA[0:8, 0:128], sctm[:, (t * NSUB + sub) * 8:(t * NSUB + sub + 1) * 8],
                idf_s[:, :]), wait=[("eV", ev3)], inc="eT")
            sc_guard = op("scalar", lambda s, sub=sub: s.copy(
                rcp[0:8, sub * 128:(sub + 1) * 128], psA[0:8, 0:128]),
                wait=[("eT", et3)], inc="eS")
        st_s = op("sync", lambda s, t=t: s.dma_start(
            out=sc_loc[:, t * TILE:(t + 1) * TILE], in_=rcp[0:8, 0:512]),
            wait=[("eS", sc_guard)], inc="dS")

    xmid_ds = cnt["dS"]
    if debug:
        op("sync", lambda s: s.dma_start(out=dbg["d_att"], in_=xt[:, :]),
           wait=[("eV", res_ev)], inc="dS")
        op("sync", lambda s: s.dma_start(out=dbg["d_sctm"], in_=sctm[:, :]),
           wait=[("eS", cnt["eS"])], inc="dS")

    if stop_after == "ph2":
        op("gpsimd", lambda g: g.nop(), wait=[("dG", cnt["dG"]), ("dS", cnt["dS"]),
                                              ("eV", cnt["eV"]), ("eS", cnt["eS"]), ("eT", cnt["eT"])])
        with nc.Block() as block:
            P.replay(block)
        ctx.close()
        return nc

    P.tag = "ph3"
    agc = op("gpsimd", lambda g: g.collective_compute(
        "AllGather", OP.bypass, replica_groups=[list(range(NCORES))],
        ins=[sc_loc], outs=[sc_ag]), wait=[("dS", cnt["dS"])], inc="cc", amt=1)
    sc_all = xt.ap()[0:64, :]
    ld_sc = op("sync", lambda s: s.dma_start(out=sc_all, in_=sc_ag),
               wait=[("cc", agc)], inc="dS")

    def vbis0(v):
        v.memset(bisv[:, 0:1], 0.0)
        return v.memset(bisv[:, 1:2], 2.0)
    ev = op("vector", vbis0, wait=[("dS", ld_sc)], inc="eV")
    cnt8 = ar(f_cnt8, [4096])[0:64, :]
    P.tag = "ph3bis"
    for it in range(NBIS):
        ev = op("vector", lambda v: v.tensor_scalar(
            bisv[:, 2:3], bisv[:, 0:1], bisv[:, 1:2], 0.5, op0=OP.add, op1=OP.mult),
            wait=[("eV", ev), ("eT", cnt["eT"])], inc="eV")
        et = op("tensor", lambda tn: tn.matmul(psB[0:64, 0:1], rep8[:, :], bisv[:, 2:3],
                                               start=True, stop=True),
                wait=[("eV", ev)], inc="eT")

        def vcnt(v):
            return v.tensor_scalar(cnt8, sc_all, psB[0:64, 0:1], 0.0,
                                   op0=OP.is_ge, op1=OP.add, accum_out=cntv[:, :])
        ev = op("vector", vcnt, wait=[("eT", et)], inc="eV")
        et = op("tensor", lambda tn: tn.matmul(psA[0:8, 0:1], fold8[:, :], cntv[:, :],
                                               start=True, stop=True),
                wait=[("eV", ev)], inc="eT")

        def vupdA(v):
            v.tensor_scalar(bisv[:, 3:4], psA[0:8, 0:1], float(KTOP), None, op0=OP.is_ge)
            v.tensor_tensor(bisv[:, 4:5], bisv[:, 2:3], bisv[:, 0:1], op=OP.subtract)
            return v.tensor_tensor(bisv[:, 6:7], bisv[:, 1:2], bisv[:, 2:3], op=OP.subtract)
        ev = op("vector", vupdA, wait=[("eT", et)], inc="eV")

        def vupdB(v):
            v.tensor_tensor(bisv[:, 5:6], bisv[:, 3:4], bisv[:, 4:5], op=OP.mult)
            return v.tensor_tensor(bisv[:, 7:8], bisv[:, 3:4], bisv[:, 6:7], op=OP.mult)
        ev = op("vector", vupdB, wait=[("eV", ev)], inc="eV")

        def vupdC(v):
            v.tensor_tensor(bisv[:, 0:1], bisv[:, 0:1], bisv[:, 5:6], op=OP.add)
            return v.tensor_tensor(bisv[:, 1:2], bisv[:, 2:3], bisv[:, 7:8], op=OP.add)
        ev = op("vector", vupdC, wait=[("eV", ev)], inc="eV")
    bis_ev = ev
    P.tag = "ph3"
    if debug:
        op("sync", lambda s: s.dma_start(out=dbg["d_tau"], in_=bisv[:, 0:2]),
           wait=[("eV", bis_ev)], inc="dS")

    ev = op("vector", lambda v: v.tensor_scalar(lns[0:8, 0:8], idf_s[0:8, 0:8], bisv[:, 0:1],
                                                None, op0=OP.mult),
            wait=[("eV", bis_ev)], inc="eV")
    op("vector", lambda v: v.memset(ones8s[:, :], 1.0), inc="eV")
    et = op("tensor", lambda tn: tn.matmul(psB[0:128, 0:8], ones8s[:, :], lns[0:8, 0:8],
                                           start=True, stop=True),
            wait=[("eV", cnt["eV"])], inc="eT")

    def vmask(v):
        last = None
        for e in range(E):
            last = v.tensor_scalar(msk.ap().rearrange("p (b e) -> p b e", e=8)[:, :, e],
                                   sctm.ap().rearrange("p (b e) -> p b e", e=8)[:, :, e],
                                   psB[0:128, e:e + 1], None, op0=OP.is_ge)
        return last
    ev = op("vector", vmask, wait=[("eT", et)], inc="eV")

    def vmg(v):
        v.tensor_copy(mskb[:, :], msk[:, :])
        return v.tensor_tensor(gte[:, :], sctm[:, :], msk[:, :], op=OP.mult)
    ev_g = op("vector", vmg, wait=[("eV", ev)], inc="eV")

    def tpos(tn):
        tn.matmul(ps1k[:, 0:256], lt_s[:, :], mskb[:, :], start=True, stop=True)
        return tn.matmul(psA[0:1, 0:256], onescol_bf[:, :], mskb[:, :], start=True, stop=True)
    et = op("tensor", tpos, wait=[("eV", ev_g)], inc="eT")
    es = op("scalar", lambda s: s.copy(bsum[:, :], psA[0:1, 0:256]),
            wait=[("eT", et)], inc="eS")

    ev = op("vector", lambda v: v.tensor_copy(
        binc.ap().rearrange("x (e b) -> x e b", b=32),
        bsum.ap().rearrange("x (b e) -> x e b", e=8)), wait=[("eS", es)], inc="eV")
    ev = op("vector", lambda v: v.tensor_tensor_scan(
        bsum[:, :], kmv[:, :], binc[:, :], 0.0, op0=OP.mult, op1=OP.add),
        wait=[("eV", ev)], inc="eV")
    ev = op("vector", lambda v: v.tensor_tensor(
        bsum[:, :], bsum[:, :], binc[:, :], op=OP.subtract),
        wait=[("eV", ev)], inc="eV")

    et = op("tensor", lambda tn: tn.matmul(
        psB[:, 0:256], ones8s[0:1, 0:128],
        bsum.ap().rearrange("x (e b) -> x b e", b=32),
        start=True, stop=True),
        wait=[("eV", cnt["eV"])], inc="eT")

    if debug:
        dbgpos_hold = ctx.enter_context(st("dbgpos", [128, 256], F32))
    evo = op("vector", lambda v: v.tensor_scalar(
        offf[:, :], msk[:, :], 1.0, -1e6, op0=OP.subtract, op1=OP.mult),
        wait=[("eT", et)], inc="eV")
    evo = op("vector", lambda v: v.tensor_tensor(
        offf[:, :], offf[:, :], ps1k[:, 0:256], op=OP.add), wait=[("eV", evo)], inc="eV")
    evo = op("vector", lambda v: v.tensor_tensor(
        offf[:, :], offf[:, :], psB[:, 0:256], op=OP.add), wait=[("eV", evo)], inc="eV")
    evo = op("vector", lambda v: v.tensor_copy(offsc[:, :], offf[:, :]),
             wait=[("eV", evo)], inc="eV")
    evo = op("vector", lambda v: v.tensor_scalar(
        offf[:, :], offf[:, :], float(CAP), None, op0=OP.min), wait=[("eV", evo)], inc="eV")
    evo = op("vector", lambda v: v.tensor_copy(offgu[:, :], offf[:, :]),
             wait=[("eV", evo)], inc="eV")
    if debug:
        def vdbgp(v):
            v.tensor_copy(dbgpos_hold[:, :], ps1k[:, 0:256])
            return v.tensor_tensor(dbgpos_hold[:, :], dbgpos_hold[:, :],
                                   psB[:, 0:256], op=OP.add)
        evo = op("vector", vdbgp, wait=[("eV", evo)], inc="eV")
    ev_off = evo
    if debug:
        op("sync", lambda s: s.dma_start(out=dbg["d_pos"], in_=dbgpos_hold[:, :]),
           wait=[("eV", ev_off)], inc="dS")

    corr_ds = 0
    for blk in range(NBLK):
        et = op("tensor", lambda tn, blk=blk: tn.transpose(
            psB[0:8, 0:128], gte.ap().rearrange("p (b e) -> p b e", e=8)[:, blk, :],
            idf_s[:, :]), wait=[("eV", ev_off), ("eS", cnt["eS"])], inc="eT")
        es = op("scalar", lambda s: s.copy(mxqf[0:8, 0:128], psB[0:8, 0:128]),
                wait=[("eT", et)], inc="eS")

        def tcor(tn):
            tn.matmul(ps1k[:, 0:512], mxqf[0:8, 0:128], b2s[:, 0:512], start=True, stop=True)
            return tn.matmul(ps1k[:, 512:1024], mxqf[0:8, 0:128], b2s[:, 512:1024],
                             start=True, stop=True)
        et2 = op("tensor", tcor, wait=[("eS", es)], inc="eT")
        es2 = op("scalar", lambda s: s.copy(ar(o_corr, [1024]), ps1k[:, :]),
                 wait=[("eT", et2)], inc="eS")
        corr_ds = op("gpsimd", lambda g, blk=blk: g.dma_start(
            out=out_d[blk * 128:(blk + 1) * 128, :], in_=ar(o_corr, [1024]),
            accum_op=OP.add), wait=[("eS", es2), ("dS", xmid_ds), ("dG", wcast)], inc="dG")

    if stop_after == "ph3":
        op("gpsimd", lambda g: g.nop(), wait=[("dG", cnt["dG"]), ("dS", cnt["dS"]),
                                              ("eV", cnt["eV"]), ("eS", cnt["eS"]), ("eT", cnt["eT"])])
        op("sync", lambda s: s.nop(), wait=[("dS", cnt["dS"]), ("dG", cnt["dG"])])
        with nc.Block() as block:
            P.replay(block)
        ctx.close()
        return nc

    P.tag = "ph4"
    w1q = ar(f_w1q, [2, 4, 8, 128])
    w2t = ar(f_w2, [32, 1024])
    xinT = ar(f_xinT, [8, 1280])
    ht = ar(f_h, [32, 512])
    ctb = ar(f_ctb, [2, 1024])
    stag = ar(f_stag, [3, 1024])
    ln2b = ar(f_ln2b, [2, 1024])

    bc_reg = {}

    def get_bc(g):
        if "r" not in bc_reg:
            bc_reg["r"] = g.to_reg(CAP - 1)
        return bc_reg["r"]

    CHUNKS = [(0, 512), (512, 1024), (1024, 1280)]
    w1q_free = [0, 0]
    xin_free = 0
    h_free_eT = 0
    gather_done_dG = {0: 0, 1: 0}
    sc_done = {}
    evac_es_prev = 0
    prev_w2_phase_eT = 0
    gelu_hist = {}
    accum_prev = 0
    op("scalar", lambda s: s.preload_act_table(AF.Gelu), wait=[("eS", cnt["eS"])])

    P.tag = "ph4scat"
    sc_done_blk = {}
    for blk in range(NBLK):
        par = blk % 2
        ldb = op("sync", lambda s, blk=blk, par=par: s.dma_start(
            out=ln2b[:, par, :], in_=ln2_h[blk * 128:(blk + 1) * 128, :]),
            wait=[("dG", sc_done_blk.get(blk - 2, 0))], inc="dS")
        for e in range(E):
            def gsc(g, e=e, blk=blk, par=par):
                return g.indirect_dma_start(
                    out=xin_hs[e],
                    out_offset=IndirectOffsetOnAxis(
                        offsc.ap().rearrange("p (b q) -> p b q", q=8)[:, blk, e:e + 1], 0),
                    in_=ln2b[:, par, :], in_offset=None,
                    bounds_check=get_bc(g), oob_is_err=False)
            op("gpsimd", gsc, wait=[("dS", ldb), ("eV", ev_off)], inc="dG")
        sc_done_blk[blk] = cnt["dG"]
    scat_all = cnt["dG"]

    for e in range(E):
        P.tag = "ph4w2ld"
        w2_ds = op("scalar", lambda s, e=e: s.dma_start(
            out=ar(f_w2, [32768]), in_=w2bf[:, e * 32768:(e + 1) * 32768]),
            wait=[("eT", prev_w2_phase_eT), ("dG", corr_ds if e == 0 else wcast)], inc="dP")

        P.tag = "ph4xpose"
        trl = 0
        for ci in range(8):
            trl = op("sync", lambda s, e=e, ci=ci: s.dma_start_transpose(
                xinT[:, ci, :], xin_hs[e][:, ci * 128:(ci + 1) * 128]),
                wait=[("dG", scat_all), ("eT", xin_free)], inc="dS")
        tr_done = trl

        w1_ds = {}

        def plan_w1_load(k):
            ic_, q_ = k // 8, k % 8
            slot_ = k % 2
            oldtag, P.tag = P.tag, "ph4w1ld"
            w1_ds[(ic_, q_)] = op("scalar", lambda s, e=e, q_=q_, slot_=slot_: s.dma_start(
                out=w1q[:, slot_, :, :, :].rearrange("p a b c -> p (a b c)"),
                in_=w1bf[:, e * 32768 + q_ * 4096:e * 32768 + (q_ + 1) * 4096]),
                wait=[("eT", w1q_free[slot_]), ("dG", wcast)], inc="dP")
            P.tag = oldtag
        plan_w1_load(0)
        plan_w1_load(1)

        gelu_es = 0
        P.tag = "ph4mm"
        for ic, (c0, c1) in enumerate(CHUNKS):
            nch = c1 - c0
            for j in range(32):
                q = j // 4
                slot = (ic * 8 + q) % 2

                def tw1(tn, e=e, j=j, slot=slot, c0=c0, c1=c1, nch=nch):
                    ps = psA if (j % 2 == 0) else psB
                    last = None
                    for ci in range(8):
                        last = tn.matmul(ps[:, 0:nch], w1q[:, slot, j % 4, ci, :],
                                         xinT[:, ci, c0:c1], start=(ci == 0), stop=(ci == 7))
                    return last
                wts = [("dP", w1_ds[(ic, q)]), ("dS", tr_done),
                       ("eS", gelu_hist.get((e, ic, j - 2), 0)),
                       ("eT", h_free_eT)]
                et = op("tensor", tw1, wait=wts, inc="eT")
                w1q_free[slot] = et
                if j % 4 == 3:
                    knext = ic * 8 + q + 2
                    if knext < 24:
                        plan_w1_load(knext)

                def sg(s, e=e, j=j, nch=nch):
                    ps = psA if (j % 2 == 0) else psB
                    return s.activation(ht[:, j, 0:nch], ps[:, 0:nch], AF.Gelu,
                                        bias=b1s[:, e * 32 + j:e * 32 + j + 1], scale=1.0)
                gelu_es = op("scalar", sg, wait=[("eT", et)], inc="eS")
                gelu_hist[(e, ic, j)] = gelu_es

            for sb in range(c0 // 128, c1 // 128):
                par2 = sb % 2

                def tw2(tn, sb=sb, c0=c0, par2=par2):
                    pps = ps1k[:, 0:1024] if par2 == 0 else ps2[:, 0:1024]
                    last = None
                    for j in range(32):
                        for half in range(2):
                            last = tn.matmul(pps[:, half * 512:(half + 1) * 512],
                                             ht[:, j, sb * 128 - c0:sb * 128 - c0 + 128],
                                             w2t[:, j, half * 512:(half + 1) * 512],
                                             start=(j == 0), stop=(j == 31))
                    return last
                et = op("tensor", tw2, wait=[("eS", gelu_es), ("dP", w2_ds),
                                             ("eS", evac_es_prev)], inc="eT")
                if sb == c1 // 128 - 1:
                    h_free_eT = et

                def sev2(s, par2=par2):
                    pps = ps1k[:, 0:1024] if par2 == 0 else ps2[:, 0:1024]
                    return s.copy(ctb[:, par2, :], pps)
                es = op("scalar", sev2, wait=[("eT", et)], inc="eS")
                evac_es_prev = es
                op("sync", lambda s, e=e, sb=sb, par2=par2: s.dma_start(
                    out=ctb_hs[e % 2][sb * 128:(sb + 1) * 128, :], in_=ctb[:, par2, :]),
                   wait=[("eS", es), ("dG", gather_done_dG[e % 2])], inc="dS")
        ctb_ds = cnt["dS"]
        prev_w2_phase_eT = cnt["eT"]
        xin_free = cnt["eT"]

        P.tag = "ph4gath"
        gl = 0
        for blk in range(NBLK):
            sl = blk % 2

            def gga(g, e=e, blk=blk, sl=sl):
                return g.indirect_dma_start(
                    out=stag[:, sl, :], out_offset=None,
                    in_=ctb_hs[e % 2],
                    in_offset=IndirectOffsetOnAxis(
                        offgu.ap().rearrange("p (b q) -> p b q", q=8)[:, blk, e:e + 1], 0))
            dg = op("gpsimd", gga, wait=[("dS", ctb_ds), ("eV", cnt["eV"])], inc="dG")

            def vgt(v, e=e, blk=blk, sl=sl):
                return v.tensor_scalar(stag[:, 2, :], stag[:, sl, :],
                                       gte.ap().rearrange("p (b q) -> p b q", q=8)[:, blk, e:e + 1],
                                       None, op0=OP.mult)
            ev2 = op("vector", vgt, wait=[("dG", dg), ("dG", accum_prev)], inc="eV")
            accum_prev = op("gpsimd", lambda g, blk=blk: g.dma_start(
                out=out_d[blk * 128:(blk + 1) * 128, :], in_=stag[:, 2, :],
                accum_op=OP.add), wait=[("eV", ev2)], inc="dG")
            gl = accum_prev
        gather_done_dG[e % 2] = gl

    if debug:
        op("gpsimd", lambda g: g.dma_start(out=dbg["d_ln2"], in_=ln2_h),
           wait=[("dG", cnt["dG"]), ("dS", cnt["dS"])], inc="dG")
        op("gpsimd", lambda g: g.dma_start(out=dbg["d_ctb0"], in_=ctb_hs[0]), inc="dG")
        op("gpsimd", lambda g: g.dma_start(out=dbg["d_ctb1"], in_=ctb_hs[1]), inc="dG")
        op("gpsimd", lambda g: g.dma_start(out=dbg["d_xin0"], in_=xin_hs[0]), inc="dG")
        op("gpsimd", lambda g: g.dma_start(out=dbg["d_xin1"], in_=xin_hs[1]), inc="dG")

    op("gpsimd", lambda g: g.nop(), wait=[("dG", cnt["dG"]), ("dS", cnt["dS"])])
    op("sync", lambda s: s.nop(), wait=[("dG", cnt["dG"])])

    with nc.Block() as block:
        P.replay(block)
    ctx.close()
    return nc


def _prep_inputs(L0_attn_x, L0_attn_w_qkv, L0_attn_b_qkv, L0_attn_w_out, L0_attn_b_out,
                 L0_attn_proj_matrix, L0_ffn_Wr, L0_ffn_W1, L0_ffn_b1, L0_ffn_W2, L0_ffn_b2):
    f32 = np.float32
    wqkv = np.ascontiguousarray(L0_attn_w_qkv, dtype=f32)
    wq_l = wqkv.reshape(8, 128, 24, 128).transpose(1, 2, 0, 3).reshape(128, 24 * 8 * 128)
    wout = np.ascontiguousarray(L0_attn_w_out, dtype=f32)
    wo_l = wout.reshape(8, 128, 8, 128).transpose(1, 0, 2, 3).reshape(128, 8 * 8 * 128)
    proj = np.asarray(L0_attn_proj_matrix, dtype=f32)
    pp = np.zeros((128, H, 2, 128), f32)
    for h in range(H):
        base = (h % 2) * 64
        for pi in range(2):
            pp[base:base + 64, h, pi, :] = proj[h, pi::2, :].T
    projp = pp.reshape(128, H * 2 * 128)
    wr = np.ascontiguousarray(L0_ffn_Wr, dtype=f32)
    wr_l = wr.reshape(8, 128, 8).transpose(1, 0, 2).reshape(128, 64)
    W1 = np.asarray(L0_ffn_W1, dtype=f32)
    w1_l = W1.reshape(8, 8, 128, 4, 8, 128).transpose(2, 0, 3, 4, 1, 5).reshape(128, E * 32768)
    W2 = np.asarray(L0_ffn_W2, dtype=f32)
    w2_l = W2.reshape(8, 32, 128, 1024).transpose(2, 0, 1, 3).reshape(128, E * 32768)
    b1_l = np.asarray(L0_ffn_b1, dtype=f32).reshape(E, 32, 128).transpose(2, 0, 1).reshape(128, E * 32)
    b2_l = np.ascontiguousarray(L0_ffn_b2, dtype=f32)
    bqkv = np.asarray(L0_attn_b_qkv, dtype=f32).reshape(1, 3 * C)
    bout = np.asarray(L0_attn_b_out, dtype=f32).reshape(1, C)

    lt = np.ascontiguousarray(np.tril(np.ones((128, 128), f32), -1).T)
    lt_bf = lt.astype(ml_dtypes.bfloat16)
    idf = np.eye(128, dtype=f32)
    oh = np.zeros((16, 16 * 128), ml_dtypes.bfloat16)
    for h in range(16):
        oh[h, h * 128:(h + 1) * 128] = 1.0
    fold = np.zeros((64, 8), f32)
    for p in range(64):
        fold[p, p % 8] = 1.0
    rp = np.zeros((8, 64), f32)
    for p in range(64):
        rp[p % 8, p] = 1.0
    km = np.ones((1, 256), f32)
    km[0, 0::32] = 0.0
    onesrb = np.ones((1, 512), ml_dtypes.bfloat16)

    shared = dict(wqkv_l=wq_l, bqkv=bqkv, wout_l=wo_l, bout=bout, projpad=projp,
                  wr_l=wr_l, w1_l=w1_l, b1_l=b1_l, w2_l=w2_l, b2_l=b2_l,
                  lt_exc=lt_bf, ident_f=idf, onehot16=oh, fold8=fold, rep8=rp,
                  killmask=km, ones_rowb=onesrb)
    x = np.asarray(L0_attn_x, dtype=f32)
    in_maps = []
    for c in range(NCORES):
        m = dict(shared)
        m["x"] = np.ascontiguousarray(x[c])
        in_maps.append(m)
    return in_maps


_NC_CACHE = {}


def kernel(**inputs):
    in_maps = _prep_inputs(**inputs)
    if "nc" not in _NC_CACHE:
        _NC_CACHE["nc"] = build_nc(debug=False)
    nc = _NC_CACHE["nc"]
    res = run_bass_kernel_spmd(nc, in_maps, core_ids=list(range(NCORES)))
    out = np.stack([np.asarray(r["out"]) for r in res.results], axis=0)
    return out
